# version 31
# baseline (speedup 1.0000x reference)
import numpy as np
import concourse.bass as bass
import concourse.mybir as mybir
from concourse.bass_isa import ReduceOp
from concourse.bass_utils import run_bass_kernel_spmd

F32 = mybir.dt.float32
BF16 = mybir.dt.bfloat16
I32 = mybir.dt.int32
AF = mybir.ActivationFunctionType
ALU = mybir.AluOpType
AX = mybir.AxisListType

ENGINES = ("tensor", "scalar", "vector", "sync", "gpsimd")

B, N, C, H, HD = 2, 8192, 1024, 16, 64
TQ = 2048
NKT = N // 128
CC = C // 128
LN_EPS = 1e-5
RMS_EPS = 1e-8
DQ = 1.0 / 127.0
HID = 3072
HC = HID // 128

NONE_MARK = (None, 0)


class Sched:

    def __init__(self, nc):
        self.nc = nc
        self.stream = {e: [] for e in ENGINES}
        self.sem = {e: nc.alloc_semaphore("s_" + e) for e in ENGINES}
        self.s_out = nc.alloc_semaphore("s_out")
        self.cnt = {e: 0 for e in ENGINES}
        self.outcnt = 0
        self.lastwait = {e: {} for e in ENGINES}

    def op(self, e, fn, deps=(), dma=False, out=False):
        rec = [fn, [], []]
        for pe, v in deps:
            if pe is None or pe == e or v <= 0:
                continue
            if self.lastwait[e].get(pe, 0) >= v:
                continue
            self.lastwait[e][pe] = v
            rec[1].append((self.s_out if pe == "out" else self.sem[pe], v))
        if out:
            self.outcnt += 16
            rec[2].append((self.s_out, 16))
        elif dma:
            self.cnt[e] += 16
            rec[2].append((self.sem[e], 16))
        self.stream[e].append(rec)
        return rec

    def out_mark(self):
        return ("out", self.outcnt)

    def mark(self, e):
        for rec in reversed(self.stream[e]):
            if rec[0] is None:
                continue
            if not any(s is self.sem[e] for s, _ in rec[2]):
                self.cnt[e] += 1
                rec[2].append((self.sem[e], 1))
            return (e, self.cnt[e])
        return (e, 0)

    def wait_only(self, e, deps):
        rec = [None, [], []]
        for pe, v in deps:
            if pe is None or pe == e or v <= 0:
                continue
            if self.lastwait[e].get(pe, 0) >= v:
                continue
            self.lastwait[e][pe] = v
            rec[1].append((self.s_out if pe == "out" else self.sem[pe], v))
        if rec[1]:
            self.stream[e].append(rec)

    def barrier(self):
        ms = [self.mark(e) for e in ENGINES]
        for e in ENGINES:
            self.wait_only(e, ms)

    def final_out_wait(self, e):
        self.stream[e].append([None, [(self.s_out, self.outcnt)], []])

    def build_block(self):
        nc = self.nc
        with nc.Block() as block:
            def runner(ename):
                def run(eng):
                    for fn, waits, incs in self.stream[ename]:
                        for sem, v in waits:
                            eng.wait_ge(sem, v)
                        if fn is not None:
                            inst = fn(eng)
                            for sem, n in incs:
                                inst.then_inc(sem, n)
                return run
            block.tensor(runner("tensor"))
            block.scalar(runner("scalar"))
            block.vector(runner("vector"))
            block.sync(runner("sync"))
            block.gpsimd(runner("gpsimd"))


class _Trunc(Exception):
    pass


def build_program(stop_after=None):
    nc = bass.Bass("TRN2", target_bir_lowering=False)

    xT = nc.dram_tensor("xT", [C, N], F32, kind="ExternalInput")
    xq = nc.dram_tensor("xq", [C, TQ], F32, kind="ExternalInput")
    wqkv = nc.dram_tensor("wqkv", [C, 3 * C], F32, kind="ExternalInput")
    wproj = nc.dram_tensor("wproj", [C, C], F32, kind="ExternalInput")
    bproj = nc.dram_tensor("bproj", [C], F32, kind="ExternalInput")
    qscale = nc.dram_tensor("qscale", [HD], F32, kind="ExternalInput")
    kscale = nc.dram_tensor("kscale", [HD], F32, kind="ExternalInput")
    wgT = nc.dram_tensor("wgT", [C, HID], I32, kind="ExternalInput")
    wuT = nc.dram_tensor("wuT", [C, HID], I32, kind="ExternalInput")
    wdT = nc.dram_tensor("wdT", [HID, C], I32, kind="ExternalInput")
    out = nc.dram_tensor("out", [C, TQ], F32, kind="ExternalOutput")
    ksp = nc.dram_tensor("ksp", [8, 128, N], BF16, kind="Internal")
    vsp = nc.dram_tensor("vsp", [8, 128, NKT * 130], BF16, kind="Internal")
    qsp = nc.dram_tensor("qsp", [8, 128, TQ], BF16, kind="Internal")

    SB_BASE = 17408
    def sb(name, shape, dtype, off):
        return nc.alloc_sbuf_tensor_at(name, shape, dtype, offset=SB_BASE + off)

    ones_bf = sb("ones_bf", [128, 128], BF16, 0)
    bones_bf = sb("bones_bf", [128, 128], BF16, 256)
    onesf32 = sb("onesf32", [128, 128], F32, 512)
    qs8col = sb("qs8col", [128, 1], F32, 1024)
    kscol = sb("kscol", [128, 1], F32, 1056)
    negB = sb("negB", [128, 1], F32, 1088)
    maxQs2 = sb("maxQs2", [128, 1], F32, 1120)
    maxK2 = sb("maxK2", [128, 1], F32, 1152)
    tmpcol = sb("tmpcol", [128, 1], F32, 1184)
    bcol = sb("bcol", [128, 1], F32, 1216)
    bpcol = sb("bpcol", [128, 8], F32, 1248)
    brow = sb("brow", [128, 128], F32, 1280)
    epsln_col = sb("epsln_col", [128, 1], F32, 1792)
    negone_col = sb("negone_col", [128, 1], F32, 1824)
    bcol2 = sb("bcol2", [128, 1], F32, 1856)
    maxQs2b = sb("maxQs2b", [128, 1], F32, 1888)
    maxK2b = sb("maxK2b", [128, 1], F32, 1920)

    Qstg = [sb(f"qstg{i}", [128, 512], BF16, 2048 + i * 1024) for i in range(2)]
    FB = 133120
    wqkv_sb = [sb(f"wq{i}", [128, 3 * C], BF16, FB + i * 6144) for i in range(8)]
    Kstg = [sb(f"kst{i}", [128, 512], BF16, 203776 + i * 1024) for i in range(2)]
    Vstg = sb("vstg", [128, 8 * 260], BF16, 205824)

    xb_pp = [sb(f"xb{i}", [128, 4096], BF16, 34816 + i * 8192) for i in range(2)]
    xtmp_pp = [sb(f"xtp{i}", [128, 4096], BF16, 51200 + i * 8192) for i in range(2)]
    rscr = sb("rscr", [128, 512], F32, 100352)
    qsq_pp = [sb(f"qsqp{i}", [128, 512], BF16, 102400 + i * 1024) for i in range(2)]
    q2_pp = [sb(f"q2p{i}", [128, 512], BF16, 104448 + i * 1024) for i in range(2)]
    sd_pp = [sb(f"sdp{i}", [128, 512], F32, 106496 + i * 2048) for i in range(2)]
    rc_pp = [sb(f"rcp{i}", [128, 512], F32, 110592 + i * 2048) for i in range(2)]
    mrow_pp = [sb(f"mrw{i}", [128, 512], F32, 114688 + i * 2048) for i in range(2)]
    lnm2_pp = [sb(f"lm2{i}", [128, 512], F32, 118784 + i * 2048) for i in range(2)]
    lnvar_pp = [sb(f"lvr{i}", [128, 512], F32, 122880 + i * 2048) for i in range(2)]
    lnsd_pp = [sb(f"lsd{i}", [128, 512], F32, 126976 + i * 2048) for i in range(2)]
    rstdb_pp = [sb(f"rsb{i}", [128, 512], BF16, 131072 + i * 1024) for i in range(2)]

    Qsb = [sb(f"qsb{i}", [128, 512], BF16, 4096 + i * 1024) for i in range(2)]
    attnout2 = [[sb(f"ao{s}_{i}", [128, 512], BF16, 6144 + (s * 8 + i) * 1024)
                 for i in range(8)] for s in range(2)]
    delta1v = [sb(f"d1v{i}", [128, 512], BF16, 22528 + i * 1024) for i in range(8)]
    xn2v = [sb(f"xn2v{i}", [128, 512], BF16, 30720 + i * 1024) for i in range(8)]
    hact = [sb(f"ha{i}", [128, 512], BF16, 38912 + i * 1024) for i in range(24)]
    sq2 = sb("sq2", [128, 4096], BF16, 38912)
    r1b = sb("r1b", [128, 4096], BF16, 63488)
    xr = sb("xr", [128, 4096], F32, 71680)
    m2 = sb("m2", [128, 512], F32, 88064)
    mm2 = sb("mm2", [128, 512], F32, 90112)
    var2 = sb("var2", [128, 512], F32, 92160)
    sd2 = sb("sd2", [128, 512], F32, 94208)
    rstd2 = sb("rstd2", [128, 512], F32, 96256)
    wproj_sb = [sb(f"wp{i}", [128, C], BF16, 98304 + i * 2048) for i in range(8)]
    wgt = [sb(f"wgt{i}", [128, C], BF16, 114688 + i * 2048) for i in range(2)]
    wut = [sb(f"wut{i}", [128, C], BF16, 118784 + i * 2048) for i in range(2)]
    Pp = [sb("pph0", [128, 1024], BF16, 2048),
          sb("pph1", [128, 1024], BF16, 131072),
          sb("pph2", [128, 1024], BF16, 190528),
          sb("pph3", [128, 1024], BF16, 192576),
          sb("ppr0", [128, 1024], BF16, 122880),
          sb("ppr1", [128, 1024], BF16, 124928)]
    rrow = sb("rrow", [128, 512], F32, 126976)
    rrow2 = sb("rrow2", [128, 512], F32, 129024)
    Vhold = sb("vhold", [128, 520], BF16, 204928)
    rden = sb("rden", [128, 1024], F32, 194624)
    rtt = sb("rtt", [128, 512], F32, 206976)
    wd_sb = [sb(f"wd{i}", [128, C], BF16, FB + i * 2048) for i in range(24)]
    Ksb2 = [sb(f"ksb2{i}", [128, 1024], BF16, 182272 + i * 2048) for i in range(2)]
    Vsb2 = [sb(f"vsb2{i}", [128, 8 * 130], BF16, 186368 + i * 2080) for i in range(2)]
    gs = sb("gs", [128, 512], BF16, 198784)
    us = sb("us", [128, 512], BF16, 199808)
    tgs = sb("tgs", [128, 512], F32, 200832)
    trc = sb("trc", [128, 512], F32, 202880)

    ps_a = nc.alloc_psum_tensor("ps_a", [128, 512], F32)
    ps_b = nc.alloc_psum_tensor("ps_b", [128, 512], F32)
    ps_c = nc.alloc_psum_tensor("ps_c", [128, 512], F32)
    ps_d = nc.alloc_psum_tensor("ps_d", [128, 512], F32)
    ps_big = nc.alloc_psum_tensor("ps_big", [128, 2048], F32)

    S = Sched(nc)

    def lp(f):
        def g(e):
            with nc.allow_low_precision(reason="bf16 stream operand"):
                return f(e)
        return g

    def ckpt(name):
        if stop_after == name:
            raise _Trunc()

    try:
        S.op("vector", lambda e: e.memset(ones_bf[:, :], 1.0))
        S.op("vector", lambda e: e.memset(bones_bf[:, :], 0.0))
        S.op("vector", lambda e: e.memset(bones_bf[0:64, 0:64], 1.0))
        S.op("vector", lambda e: e.memset(bones_bf[64:128, 64:128], 1.0))
        S.op("vector", lambda e: e.memset(onesf32[:, :], 1.0))
        S.op("vector", lambda e: e.memset(
            Vstg[:].rearrange("p (n c) -> p n c", c=65)[:, :, 64:65], 1.0))
        S.op("vector", lambda e: e.memset(epsln_col[:, 0:1], LN_EPS))
        S.op("vector", lambda e: e.memset(negone_col[:, 0:1], -1.0))
        m_const = S.mark("vector")

        S.op("sync", lambda e: e.dma_start(
            out=qs8col[0:64, 0:1], in_=qscale[:].rearrange("(p o) -> p o", o=1)), dma=True)
        S.op("sync", lambda e: e.dma_start(
            out=qs8col[64:128, 0:1], in_=qscale[:].rearrange("(p o) -> p o", o=1)), dma=True)
        S.op("sync", lambda e: e.dma_start(
            out=kscol[0:64, 0:1], in_=kscale[:].rearrange("(p o) -> p o", o=1)), dma=True)
        S.op("sync", lambda e: e.dma_start(
            out=kscol[64:128, 0:1], in_=kscale[:].rearrange("(p o) -> p o", o=1)), dma=True)
        S.op("sync", lambda e: e.dma_start(
            out=bpcol[:, 0:8], in_=bproj[:].rearrange("(f p) -> p f", p=128),
            allow_slow_non_contiguous=True), dma=True)
        m_cols = S.mark("sync")
        S.op("vector", lambda e: e.tensor_scalar_mul(qs8col[:, 0:1], qs8col[:, 0:1], 0.125),
             deps=[m_cols])
        m_cols2 = S.mark("vector")

        S.op("vector", lambda e: e.scalar_tensor_tensor(
            bcol[:, 0:1], qs8col[:, 0:1], -1.0, qs8col[:, 0:1],
            op0=ALU.mult, op1=ALU.max), deps=[m_cols2])
        S.op("vector", lambda e: e.scalar_tensor_tensor(
            bcol2[:, 0:1], kscol[:, 0:1], -1.0, kscol[:, 0:1],
            op0=ALU.mult, op1=ALU.max), deps=[m_cols])
        m_abs = S.mark("vector")
        S.op("sync", lambda e: e.dma_start(out=brow[0:1, 0:128], in_=bcol[:, 0:1],
                                           allow_slow_non_contiguous=True),
             deps=[m_abs], dma=True)
        S.op("sync", lambda e: e.dma_start(out=brow[1:2, 0:128], in_=bcol2[:, 0:1],
                                           allow_slow_non_contiguous=True), dma=True)
        m_brow = S.mark("sync")
        S.op("vector", lambda e: e.reduce_max(tmpcol[0:2, 0:1], brow[0:2, 0:128],
                                              axis=AX.X), deps=[m_brow])
        m_red = S.mark("vector")
        S.op("sync", lambda e: e.dma_start(out=brow[0:1, 1:2], in_=tmpcol[1:2, 0:1]),
             deps=[m_red], dma=True)
        m_mv = S.mark("sync")
        S.op("vector", lambda e: e.tensor_mul(bcol2[0:1, 0:1], tmpcol[0:1, 0:1],
                                              brow[0:1, 1:2]), deps=[m_mv])
        S.op("vector", lambda e: e.tensor_scalar(
            bcol2[0:1, 0:1], bcol2[0:1, 0:1], -float(HD), -1.0,
            op0=ALU.mult, op1=ALU.add))
        m_b4 = S.mark("vector")
        S.op("tensor", lambda e: e.matmul(ps_a[:, 0:1], onesf32[0:1, 0:128],
                                          bcol2[0:1, 0:1], start=True, stop=True),
             deps=[m_b4, m_const])
        m_b5 = S.mark("tensor")
        S.op("vector", lambda e: e.tensor_copy(negB[:, 0:1], ps_a[:, 0:1]), deps=[m_b5])
        m_negB = S.mark("vector")
        stats_free = [S.mark("vector")]

        m_wq_cc = []
        for cc in range(CC):
            S.op("gpsimd", lambda e, cc=cc: e.dma_start(
                out=wqkv_sb[cc][:, :], in_=wqkv[cc * 128:(cc + 1) * 128, :]), dma=True)
            m_wq_cc.append(S.mark("gpsimd"))
        m_wqkv = m_wq_cc[-1]
        ckpt("P0")

        proj_banks = [ps_c[:, :], ps_big[:, 0:512], ps_big[:, 512:1024],
                      ps_big[:, 1024:1536]]
        proj_free = [[NONE_MARK] for _ in range(4)]
        bones_ms = ps_d[:, :]
        bones_bd = ps_big[:, 1536:2048]
        bones_ms_free = [NONE_MARK]
        bones_bd_free = [NONE_MARK]
        stats_free = [m_negB]
        rr = [0]
        ss = [0]
        slot_free = [{}, {}]
        rscr_rd = [NONE_MARK]

        def emit_ln(load_mark, xp):
            xb, xtmp = xb_pp[xp], xtmp_pp[xp]
            mrow, lnm2, lnvar = mrow_pp[xp], lnm2_pp[xp], lnvar_pp[xp]
            lnsd, rstd_b = lnsd_pp[xp], rstdb_pp[xp]
            S.op("scalar", lambda e: e.activation(xtmp[:, :], xb[:, :], AF.Square),
                 deps=[load_mark, m_const])
            m_sq = S.mark("scalar")
            for cc in range(CC):
                S.op("tensor", lambda e, cc=cc: e.matmul(
                    ps_a[:, :], ones_bf[:, :], xb[:, cc * 512:(cc + 1) * 512],
                    start=(cc == 0), stop=(cc == CC - 1)),
                    deps=[load_mark, m_const] + stats_free)
            for cc in range(CC):
                S.op("tensor", lambda e, cc=cc: e.matmul(
                    ps_b[:, :], ones_bf[:, :], xtmp[:, cc * 512:(cc + 1) * 512],
                    start=(cc == 0), stop=(cc == CC - 1)), deps=[m_sq])
            m_sums = S.mark("tensor")
            S.op("vector", lambda e: e.tensor_scalar_mul(mrow[:, :], ps_a[:, :], 1.0 / C),
                 deps=[m_sums])
            m_mrow = S.mark("vector")
            S.op("vector", lambda e: e.tensor_mul(lnm2[:, :], mrow[:, :], mrow[:, :]))
            S.op("vector", lambda e: e.scalar_tensor_tensor(
                lnvar[:, :], ps_b[:, :], 1.0 / C, lnm2[:, :],
                op0=ALU.mult, op1=ALU.subtract), deps=[m_sums])
            m_var = S.mark("vector")
            stats_free[:] = [m_var]
            S.op("scalar", lambda e: e.activation(
                lnsd[:, :], lnvar[:, :], AF.Sqrt, bias=epsln_col[:, 0:1]),
                deps=[m_var, m_const])
            m_sd = S.mark("scalar")
            S.op("vector", lp(lambda e: e.reciprocal(rstd_b[:, :], lnsd[:, :])),
                 deps=[m_sd])
            m_rstd = S.mark("vector")
            for cc in range(CC):
                sl = slice(cc * 512, (cc + 1) * 512)
                S.op("gpsimd", lambda e, sl=sl: e.tensor_sub(
                    xtmp[:, sl], xb[:, sl], mrow[:, :]), deps=[m_sums, m_mrow])
            m_cent = S.mark("gpsimd")
            for cc in range(CC):
                sl = slice(cc * 512, (cc + 1) * 512)
                S.op("vector", lambda e, sl=sl: e.tensor_mul(
                    xb[:, sl], xtmp[:, sl], rstd_b[:, :]),
                    deps=[m_cent, m_rstd])
            return S.mark("vector")

        def emit_rms(dst, scol, m_proj, pb, extra_out_deps=()):
            s = ss[0] % 2
            ss[0] += 1
            pp = proj_banks[pb]
            qsq, q2, sd, rc = qsq_pp[s], q2_pp[s], sd_pp[s], rc_pp[s]
            sf = slot_free[s]
            S.op("scalar", lambda e: e.activation(qsq[:, :], pp, AF.Square),
                 deps=[m_proj] + sf.get("qsq", []))
            m_cp = S.mark("scalar")
            S.op("tensor", lambda e: e.matmul(bones_ms, bones_bf[:, :], qsq[:, :],
                                              start=True, stop=True),
                 deps=[m_cp] + bones_ms_free)
            m_bo = S.mark("tensor")
            sf["qsq"] = [m_bo]
            S.op("scalar", lambda e: e.activation(
                sd[:, :], bones_ms, AF.Sqrt, scale=1.0 / HD),
                deps=[m_bo] + sf.get("sd", []))
            m_r = S.mark("scalar")
            bones_ms_free[:] = [m_r]
            S.op("vector", lambda e: e.tensor_scalar_add(rc[:, :], sd[:, :], RMS_EPS),
                 deps=[m_r])
            S.op("vector", lambda e: e.tensor_scalar_mul(sd[:, :], pp, scol[:, 0:1]),
                 deps=[m_proj])
            m_tmp = S.mark("vector")
            S.op("vector", lambda e: e.reciprocal(rscr[:, :], rc[:, :]),
                 deps=rscr_rd)
            m_rcp = S.mark("vector")
            S.op("gpsimd", lambda e, dst=dst: e.tensor_mul(dst, sd[:, :], rscr[:, :]),
                 deps=[m_rcp, m_tmp] + list(extra_out_deps))
            m_dst = S.mark("gpsimd")
            rscr_rd[:] = [m_dst]
            sf["sd"] = [m_dst]
            proj_free[pb][:] = [m_cp, m_tmp]
            return m_dst

        xb_consumed = [[], []]
        pendq = None
        def emit_load(t0, src, xp, extra=()):
            S.op("gpsimd", lambda e, t0=t0, src=src, xp=xp: e.dma_start(
                out=xb_pp[xp][:].rearrange("p (cc t) -> p cc t", cc=CC),
                in_=src[:, t0:t0 + 512].rearrange("(cc p) t -> p cc t", p=128)),
                deps=xb_consumed[xp] + list(extra), dma=True)
            return S.mark("gpsimd")

        qstg_last = {}

        def flush_q(pq):
            pfc, pt0, ppar, pm_proj, ppb = pq
            extra = [qstg_last[ppar]] if ppar in qstg_last else []
            m_dst = emit_rms(Qstg[ppar][:, :], qs8col, pm_proj, ppb,
                             extra_out_deps=extra)
            S.op("sync", lambda e, pfc=pfc, pt0=pt0, ppar=ppar: e.dma_start(
                out=qsp[pfc, :, pt0:pt0 + 512], in_=Qstg[ppar][:, :]),
                deps=[m_dst], dma=True)
            qstg_last[ppar] = S.mark("sync")

        load_marks = {}
        ln_marks = {}
        load_marks[0] = emit_load(0, xq, 0)
        ln_marks[0] = emit_ln(load_marks[0], 0)
        for tc in range(TQ // 512):
            t0 = tc * 512
            xp = tc % 2
            if tc + 1 < TQ // 512:
                load_marks[tc + 1] = emit_load(t0 + 512, xq, (tc + 1) % 2)
                ln_marks[tc + 1] = emit_ln(load_marks[tc + 1], (tc + 1) % 2)
            m_xn = ln_marks[tc]
            xb = xb_pp[xp]
            for fc in range(CC):
                pb = rr[0] % 4
                rr[0] += 1
                for cc in range(CC):
                    S.op("tensor", lambda e, cc=cc, fc=fc, pb=pb, xb=xb: e.matmul(
                        proj_banks[pb], wqkv_sb[cc][:, fc * 128:(fc + 1) * 128],
                        xb[:, cc * 512:(cc + 1) * 512],
                        start=(cc == 0), stop=(cc == CC - 1)),
                        deps=[m_xn, m_wq_cc[cc]] + proj_free[pb])
                m_proj = S.mark("tensor")
                if pendq is not None:
                    flush_q(pendq)
                pendq = (fc, t0, fc % 2, m_proj, pb)
            xb_consumed[xp] = [S.mark("tensor"), S.mark("vector"), S.mark("gpsimd"),
                               S.mark("scalar")]
        flush_q(pendq)
        pendq = None

        ckpt("P1")

        kstg_last = {}
        vstg_last = {}
        load_marks = {}
        ln_marks = {}
        load_marks[0] = emit_load(0, xT, 0, extra=xb_consumed[0])
        ln_marks[0] = emit_ln(load_marks[0], 0)
        for tc in range(N // 512):
            t0 = tc * 512
            xp = tc % 2
            if tc + 1 < N // 512:
                ex = xb_consumed[(tc + 1) % 2] if tc == 0 else ()
                load_marks[tc + 1] = emit_load(t0 + 512, xT, (tc + 1) % 2, extra=ex)
                ln_marks[tc + 1] = emit_ln(load_marks[tc + 1], (tc + 1) % 2)
            m_xn = ln_marks[tc]
            xb = xb_pp[xp]
            pendk = None

            def flush_k(pk):
                pfc, pt0, ppar, pm_proj, ppb = pk
                extra = [kstg_last[ppar]] if ppar in kstg_last else []
                m_dst = emit_rms(Kstg[ppar][:, :], kscol, pm_proj, ppb,
                                 extra_out_deps=extra)
                S.op("sync", lambda e, pfc=pfc, pt0=pt0, ppar=ppar: e.dma_start(
                    out=ksp[pfc, :, pt0:pt0 + 512], in_=Kstg[ppar][:, :]),
                    deps=[m_dst], dma=True)
                kstg_last[ppar] = S.mark("sync")

            for fc in range(CC):
                pb = rr[0] % 4
                rr[0] += 1
                for cc in range(CC):
                    S.op("tensor", lambda e, cc=cc, fc=fc, pb=pb, xb=xb: e.matmul(
                        proj_banks[pb], wqkv_sb[cc][:, C + fc * 128: C + (fc + 1) * 128],
                        xb[:, cc * 512:(cc + 1) * 512],
                        start=(cc == 0), stop=(cc == CC - 1)),
                        deps=[m_xn] + proj_free[pb])
                m_proj = S.mark("tensor")
                if pendk is not None:
                    flush_k(pendk)
                pendk = (fc, t0, fc % 2, m_proj, pb)
            flush_k(pendk)
            for vg in range(2):
                for tt in range(4):
                    kt = tc * 4 + tt
                    c0 = 2 * C + vg * 512
                    pb = rr[0] % 4
                    rr[0] += 1
                    for cc in range(CC):
                        S.op("tensor", lambda e, cc=cc, c0=c0, tt=tt, pb=pb, xb=xb: e.matmul(
                            proj_banks[pb],
                            xb[:, cc * 512 + tt * 128: cc * 512 + tt * 128 + 128],
                            wqkv_sb[cc][:, c0:c0 + 512],
                            start=(cc == 0), stop=(cc == CC - 1)),
                            deps=[m_xn] + proj_free[pb])
                    m_v = S.mark("tensor")
                    parity = (tt // 2) % 2
                    half = tt % 2
                    dep_sp = [m for r in range(4)
                              for m in ([vstg_last[(r, parity)]]
                                        if (r, parity) in vstg_last else [])]
                    out5 = Vstg[:, parity * 1040:(parity + 1) * 1040].rearrange(
                        "p (r hf h c) -> p r hf h c", r=4, hf=2, h=2)[
                        :, :, half:half + 1, :, 0:64]
                    S.op("scalar", lambda e, out5=out5, pb=pb: e.copy(
                        out=out5,
                        in_=proj_banks[pb].rearrange(
                            "p (r o h c) -> p r o h c", r=4, o=1, h=2)),
                        deps=[m_v, m_const] + dep_sp)
                    m_cp = S.mark("scalar")
                    proj_free[pb][:] = [m_cp]
                    if half == 1:
                        for r in range(4):
                            stg0 = parity * 4 + r
                            reg = 4 * vg + r
                            S.op("sync", lambda e, reg=reg, kt=kt, stg0=stg0: e.dma_start(
                                out=vsp[reg, :, (kt - 1) * 130:(kt + 1) * 130],
                                in_=Vstg[:, stg0 * 260: stg0 * 260 + 260]),
                                deps=[m_cp], dma=True)
                            vstg_last[(r, parity)] = S.mark("sync")
            xb_consumed[xp] = [S.mark("tensor"), S.mark("vector"), S.mark("gpsimd"),
                               S.mark("scalar")]

        ckpt("P2")
        m_kspill = S.mark("sync")
        m_vspill = S.mark("sync")

        ckpt("B")
        S.barrier()

        for cc in range(CC):
            S.op("gpsimd", lambda e, cc=cc: e.dma_start(
                out=wproj_sb[cc][:, :], in_=wproj[cc * 128:(cc + 1) * 128, :]), dma=True)
        m_wproj = S.mark("gpsimd")
        for i in range(24):
            S.op("gpsimd", lambda e, i=i: e.dma_start(
                out=wd_sb[i][:, :], in_=wdT[i * 128:(i + 1) * 128, :]), dma=True)
        m_wd = S.mark("gpsimd")

        tbank = [ps_c, ps_d]
        tb_free = [[NONE_MARK], [NONE_MARK]]
        ao_done = {}
        ao_free = {0: [NONE_MARK], 1: [NONE_MARK]}
        d1_free = [NONE_MARK]
        xr_free = [NONE_MARK]
        xn_free = [NONE_MARK]
        hact_free = [NONE_MARK]
        r1b_free = [NONE_MARK]
        m_tgs_rd = [NONE_MARK]
        m_us_rd = [NONE_MARK]
        wg_last = {}

        def tail_gen(qcp):
            slot = qcp % 2
            t0 = qcp * 512
            S.op("sync", lambda e, t0=t0: e.dma_start(
                out=xr[:].rearrange("p (cc t) -> p cc t", cc=CC),
                in_=xq[:, t0:t0 + 512].rearrange("(cc p) t -> p cc t", p=128)),
                deps=[xr_free[0]], dma=True)
            m_xr = S.mark("sync")
            yield
            d1w = d1_free[0]
            for fo in range(CC):
                bk = fo % 2
                for cc in range(CC):
                    S.op("tensor", lambda e, cc=cc, fo=fo, bk=bk, slot=slot: e.matmul(
                        tbank[bk][:, :], wproj_sb[cc][:, fo * 128:(fo + 1) * 128],
                        attnout2[slot][cc][:, :],
                        start=(cc == 0), stop=(cc == CC - 1)),
                        deps=[m_wproj, ao_done[qcp]] + tb_free[bk])
                    if cc % 2 == 1:
                        yield
                m_p = S.mark("tensor")
                S.op("vector", lambda e, fo=fo, bk=bk: e.tensor_scalar_add(
                    delta1v[fo][:, :], tbank[bk][:, :], bpcol[:, fo:fo + 1]),
                    deps=[m_p, m_cols, d1w])
                tb_free[bk][:] = [S.mark("vector")]
            ao_free[slot][:] = [S.mark("tensor")]
            m_d1 = S.mark("vector")
            for cc in range(CC):
                sl = slice(cc * 512, (cc + 1) * 512)
                S.op("vector", lambda e, cc=cc, sl=sl: e.tensor_add(
                    r1b[:, sl], xr[:, sl], delta1v[cc][:, :]),
                    deps=[m_xr, m_d1, r1b_free[0]])
            m_r1 = S.mark("vector")
            for cc in range(CC):
                S.op("tensor", lambda e, cc=cc: e.matmul(
                    ps_c[:, :], ones_bf[:, :], r1b[:, cc * 512:(cc + 1) * 512],
                    start=(cc == 0), stop=(cc == CC - 1)),
                    deps=[m_r1] + tb_free[0])
                if cc % 2 == 1:
                    yield
            m_sum1 = S.mark("tensor")
            S.op("vector", lambda e: e.tensor_mul(sq2[:, :], r1b[:, :], r1b[:, :]),
                 deps=[m_r1, hact_free[0]])
            m_sq = S.mark("vector")
            yield
            for cc in range(CC):
                S.op("tensor", lambda e, cc=cc: e.matmul(
                    ps_d[:, :], ones_bf[:, :], sq2[:, cc * 512:(cc + 1) * 512],
                    start=(cc == 0), stop=(cc == CC - 1)),
                    deps=[m_sq] + tb_free[1])
                if cc % 2 == 1:
                    yield
            m_sums = S.mark("tensor")
            S.op("vector", lambda e: e.tensor_scalar_mul(m2[:, :], ps_c[:, :], 1.0 / C),
                 deps=[m_sums, m_sum1])
            S.op("vector", lambda e: e.tensor_mul(mm2[:, :], m2[:, :], m2[:, :]))
            S.op("vector", lambda e: e.scalar_tensor_tensor(
                var2[:, :], ps_d[:, :], 1.0 / C, mm2[:, :],
                op0=ALU.mult, op1=ALU.subtract), deps=[m_sums])
            m_var = S.mark("vector")
            tb_free[0][:] = [m_var]
            tb_free[1][:] = [m_var]
            S.op("scalar", lambda e: e.activation(sd2[:, :], var2[:, :], AF.Sqrt,
                                                  bias=epsln_col[:, 0:1]), deps=[m_var])
            m_sd = S.mark("scalar")
            S.op("vector", lambda e: e.reciprocal(rstd2[:, :], sd2[:, :]), deps=[m_sd])
            S.op("vector", lambda e: e.tensor_scalar_mul(rstd2[:, :], rstd2[:, :], DQ))
            m_rstd = S.mark("vector")
            xnw = xn_free[0]
            for cc in range(CC):
                sl = slice(cc * 512, (cc + 1) * 512)
                S.op("vector", lambda e, sl=sl: e.tensor_sub(
                    sq2[:, sl], r1b[:, sl], m2[:, :]), deps=[m_rstd])
                S.op("vector", lambda e, cc=cc, sl=sl: e.tensor_mul(
                    xn2v[cc][:, :], sq2[:, sl], rstd2[:, :]), deps=[xnw])
            m_xn2 = S.mark("vector")
            r1b_free[0] = m_xn2
            yield
            hact_w = [NONE_MARK] * HC
            for hc in range(HC):
                buf = hc % 2
                ld = [wg_last[buf]] if buf in wg_last else []
                S.op("gpsimd", lambda e, hc=hc, buf=buf: e.dma_start(
                    out=wgt[buf][:].rearrange("p (cc h) -> p cc h", cc=CC),
                    in_=wgT[:, hc * 128:(hc + 1) * 128].rearrange(
                        "(cc p) h -> p cc h", p=128)), deps=ld, dma=True)
                S.op("gpsimd", lambda e, hc=hc, buf=buf: e.dma_start(
                    out=wut[buf][:].rearrange("p (cc h) -> p cc h", cc=CC),
                    in_=wuT[:, hc * 128:(hc + 1) * 128].rearrange(
                        "(cc p) h -> p cc h", p=128)), dma=True)
                m_w = S.mark("gpsimd")
                for cc in range(CC):
                    S.op("tensor", lambda e, cc=cc, buf=buf: e.matmul(
                        ps_c[:, :], wgt[buf][:, cc * 128:(cc + 1) * 128],
                        xn2v[cc][:, :], start=(cc == 0), stop=(cc == CC - 1)),
                        deps=[m_w, m_xn2] + tb_free[0])
                    if cc % 2 == 1:
                        yield
                m_g = S.mark("tensor")
                for cc in range(CC):
                    S.op("tensor", lambda e, cc=cc, buf=buf: e.matmul(
                        ps_d[:, :], wut[buf][:, cc * 128:(cc + 1) * 128],
                        xn2v[cc][:, :], start=(cc == 0), stop=(cc == CC - 1)),
                        deps=[m_w] + tb_free[1])
                    if cc % 2 == 1:
                        yield
                m_u = S.mark("tensor")
                wg_last[buf] = m_u
                S.op("vector", lp(lambda e: e.tensor_copy(gs[:, :], ps_c[:, :])),
                     deps=[m_g, m_us_rd[0]])
                m_gc = S.mark("vector")
                tb_free[0][:] = [m_gc]
                S.op("vector", lp(lambda e: e.tensor_scalar_mul(
                    us[:, :], ps_d[:, :], DQ)), deps=[m_u])
                tb_free[1][:] = [S.mark("vector")]
                S.op("scalar", lambda e: e.activation(tgs[:, :], gs[:, :], AF.Exp,
                                                      scale=-1.0),
                     deps=[m_gc, m_tgs_rd[0]])
                m_e = S.mark("scalar")
                S.op("vector", lambda e: e.tensor_scalar_add(tgs[:, :], tgs[:, :], 1.0),
                     deps=[m_e])
                S.op("vector", lambda e: e.reciprocal(trc[:, :], tgs[:, :]))
                S.op("vector", lambda e: e.tensor_mul(tgs[:, :], gs[:, :], trc[:, :]))
                S.op("vector", lambda e, hc=hc: e.tensor_mul(
                    hact[hc][:, :], tgs[:, :], us[:, :]),
                    deps=[hact_free[0]])
                hact_w[hc] = S.mark("vector")
                m_tgs_rd[0] = hact_w[hc]
                m_us_rd[0] = m_e
            for dc in range(CC):
                bk = dc % 2
                for hc in range(HC):
                    S.op("tensor", lambda e, hc=hc, dc=dc, bk=bk: e.matmul(
                        tbank[bk][:, :], wd_sb[hc][:, dc * 128:(dc + 1) * 128],
                        hact[hc][:, :], start=(hc == 0), stop=(hc == HC - 1)),
                        deps=[m_wd, hact_w[hc]] + tb_free[bk])
                    if hc % 2 == 1:
                        yield
                m_dn = S.mark("tensor")
                S.op("vector", lambda e, dc=dc: e.tensor_add(
                    m2[:, :], xr[:, dc * 512:(dc + 1) * 512], delta1v[dc][:, :]),
                    deps=[m_dn])
                S.op("vector", lambda e, bk=bk: e.tensor_add(
                    mm2[:, :], tbank[bk][:, :], m2[:, :]))
                m_o = S.mark("vector")
                tb_free[bk][:] = [m_o]
                S.op("sync", lambda e, dc=dc, t0=t0: e.dma_start(
                    out=out[dc * 128:(dc + 1) * 128, t0:t0 + 512], in_=mm2[:, :]),
                    deps=[m_o], out=True)
                S.wait_only("vector", [S.out_mark()])
            m_te_v = S.mark("vector")
            m_te_t = S.mark("tensor")
            d1_free[0] = m_te_v
            xr_free[0] = m_te_v
            xn_free[0] = m_te_t
            hact_free[0] = m_te_t

        def drain(g, n):
            if g is None:
                return None
            for _ in range(n):
                try:
                    next(g)
                except StopIteration:
                    return None
            return g

        NCK = 8
        CHUNKS = [(qcx, fcx, ckx) for qcx in range(4) for fcx in range(8)
                  for ckx in range(NCK)]
        kfree = [[NONE_MARK], [NONE_MARK]]
        vfree = [[NONE_MARK], [NONE_MARK]]
        qsb_free = [[NONE_MARK], [NONE_MARK]]
        loadmark = {}

        def issue_load(g):
            if g >= len(CHUNKS) or g in loadmark:
                return
            _, fcx, ckx = CHUNKS[g]
            b = g % 2
            S.op("sync", lambda e, fcx=fcx, ckx=ckx, b=b: e.dma_start(
                out=Ksb2[b][:, :], in_=ksp[fcx, :, ckx * 1024:(ckx + 1) * 1024]),
                deps=[m_kspill] + kfree[b], dma=True)
            S.op("sync", lambda e, fcx=fcx, ckx=ckx, b=b: e.dma_start(
                out=Vsb2[b][:, :], in_=vsp[fcx, :, ckx * 1040:(ckx + 1) * 1040]),
                deps=[m_vspill] + vfree[b], dma=True)
            loadmark[g] = S.mark("sync")

        exp_mark = {0: NONE_MARK, 1: NONE_MARK}
        pp_free = {i: NONE_MARK for i in range(6)}
        evac_ref = [NONE_MARK]
        stash = []
        vhold_free = [NONE_MARK]
        issue_load(0)
        issue_load(1)

        def pslot(kt):
            return kt if kt < 4 else 4 + (kt % 2)

        gen = None
        for qc in range(4):
            slot = qc % 2
            if qc > 0:
                gen = tail_gen(qc - 1)
            for fc in range(8):
                g0 = (qc * 8 + fc) * NCK
                qb = fc % 2
                S.op("sync", lambda e, fc=fc, qc=qc, qb=qb: e.dma_start(
                    out=Qsb[qb][:, :], in_=qsp[fc, :, qc * 512:(qc + 1) * 512]),
                    deps=[m_kspill] + qsb_free[qb], dma=True)
                S.op("sync", lambda e, fc=fc: e.dma_start(
                    out=Vhold[:, :], in_=vsp[fc, :, 0:520]),
                    deps=[m_vspill, vhold_free[0]], dma=True)
                m_q = S.mark("sync")
                exps = [NONE_MARK] * NKT

                def emit_av(ktav, deps_extra=()):
                    ps = pslot(ktav)
                    bv = (g0 + ktav // NCK) % 2
                    vo = (ktav % NCK) * 130
                    vsrc = Vhold if ktav < 4 else Vsb2[bv]
                    if ktav < 4:
                        vo = ktav * 130
                    S.op("tensor", lambda e, vsrc=vsrc, vo=vo, ktav=ktav, ps=ps: e.matmul(
                        ps_a[0:65, :], vsrc[:, vo:vo + 65],
                        Pp[ps][:, 0:512], start=(ktav == 4), stop=(ktav == 3)),
                        deps=[exps[ktav], loadmark[g0 + ktav // NCK]]
                        + list(deps_extra))
                    S.op("tensor", lambda e, vsrc=vsrc, vo=vo, ktav=ktav, ps=ps: e.matmul(
                        ps_b[0:65, :], vsrc[:, vo + 65:vo + 130],
                        Pp[ps][:, 512:1024], start=(ktav == 4), stop=(ktav == 3)))
                    m = S.mark("tensor")
                    pp_free[ps] = m
                    if ktav % NCK == NCK - 1:
                        vfree[bv][:] = [m]
                    return m

                for kt in range(NKT):
                    ck = kt // NCK
                    b = (g0 + ck) % 2
                    half = kt % 2
                    hb = half * 1024
                    ko = (kt % NCK) * 128
                    S.op("tensor", lambda e, b=b, ko=ko, hb=hb, qb=qb: e.matmul(
                        ps_big[:, hb:hb + 512],
                        Ksb2[b][0:64, ko:ko + 128],
                        Qsb[qb][0:64, :], start=True, stop=True),
                        deps=[m_q, loadmark[g0 + ck], exp_mark[half]])
                    S.op("tensor", lambda e, b=b, ko=ko, hb=hb, qb=qb: e.matmul(
                        ps_big[:, hb + 512:hb + 1024],
                        Ksb2[b][64:128, ko:ko + 128],
                        Qsb[qb][64:128, :], start=True, stop=True))
                    m_st = S.mark("tensor")
                    if kt % NCK == NCK - 1:
                        kfree[b][:] = [m_st]
                    if kt in (4, 5) and stash:
                        stash.pop(0)()
                    if kt >= 5:
                        emit_av(kt - 1,
                                deps_extra=([evac_ref[0]] if kt - 1 == 4 else ()))
                    if kt % NCK == 0:
                        issue_load(g0 + ck + 1)
                    gen = drain(gen, 1)
                    S.op("scalar", lambda e, ps=pslot(kt), hb=hb: e.activation(
                        Pp[ps][:, :], ps_big[:, hb:hb + 1024], AF.Exp,
                        bias=negB[:, 0:1]), deps=[m_st, pp_free[pslot(kt)]])
                    exp_mark[half] = S.mark("scalar")
                    exps[kt] = exp_mark[half]
                emit_av(NKT - 1)
                emit_av(0)
                emit_av(1)
                emit_av(2)
                m_av = emit_av(3)
                m_vh = S.mark("tensor")
                vhold_free[0] = m_vh
                qsb_free[qb][:] = [m_vh]
                S.op("vector", lambda e: e.tensor_copy(
                    rrow[64:65, :], ps_a[64:65, :]), deps=[m_av])
                S.op("vector", lambda e: e.tensor_copy(
                    rrow2[64:65, :], ps_b[64:65, :]))
                for u in range(2):
                    bank = (ps_a, ps_b)[u]
                    rbuf = (rrow, rrow2)[u]
                    rs = slice(u * 512, u * 512 + 512)
                    S.op("vector", lambda e, rbuf=rbuf, rs=rs: e.reciprocal(
                        rden[64:65, rs], rbuf[64:65, :]))
                    S.op("vector", lambda e, bank=bank, rbuf=rbuf: e.tensor_copy(
                        rbuf[0:64, :], bank[0:64, :]))
                    m_rc = S.mark("vector")

                    def mk_bc(u=u, bank=bank, rbuf=rbuf, rs=rs, m_rc=m_rc, fc=fc,
                              slot=slot, aof=list(ao_free[slot])):
                        def emit():
                            S.op("tensor", lambda e, rs=rs: e.matmul(
                                bank[0:64, :], onesf32[64:65, 0:64],
                                rden[64:65, rs], start=True, stop=True),
                                deps=[m_rc])
                            m_bc = S.mark("tensor")
                            S.op("vector", lambda e: e.tensor_mul(
                                attnout2[slot][fc][u * 64:(u + 1) * 64, :],
                                rbuf[0:64, :], bank[0:64, :]),
                                deps=[m_bc] + aof)
                            evac_ref[0] = S.mark("vector")
                        return emit
                    stash.append(mk_bc())
            while stash:
                stash.pop(0)()
            ao_done[qc] = S.mark("vector")
            ckpt("P3q%d" % qc)
        while gen is not None:
            gen = drain(gen, 16)
        gen = tail_gen(3)
        while gen is not None:
            gen = drain(gen, 16)
        ckpt("P6")


    except _Trunc:
        pass
    S.final_out_wait("sync")
    S.build_block()
    return nc


_NC = None


def kernel(L0_attn_x, L0_attn_Wqkv, L0_attn_Wproj, L0_attn_bproj,
           L0_attn_q_scale, L0_attn_k_scale,
           L0_ffn_w_gate, L0_ffn_w_up, L0_ffn_w_down):
    global _NC
    x = np.asarray(L0_attn_x, dtype=np.float32)
    wqkv_h = np.ascontiguousarray(np.asarray(L0_attn_Wqkv, dtype=np.float32))
    wproj_h = np.ascontiguousarray(np.asarray(L0_attn_Wproj, dtype=np.float32))
    bproj_h = np.ascontiguousarray(np.asarray(L0_attn_bproj, dtype=np.float32))
    qs_h = np.ascontiguousarray(np.asarray(L0_attn_q_scale, dtype=np.float32))
    ks_h = np.ascontiguousarray(np.asarray(L0_attn_k_scale, dtype=np.float32))
    wgT_h = np.ascontiguousarray(np.asarray(L0_ffn_w_gate, dtype=np.int32).T)
    wuT_h = np.ascontiguousarray(np.asarray(L0_ffn_w_up, dtype=np.int32).T)
    wdT_h = np.ascontiguousarray(np.asarray(L0_ffn_w_down, dtype=np.int32).T)
    xT_h = np.ascontiguousarray(x.transpose(0, 2, 1))

    if _NC is None:
        _NC = build_program()

    in_maps = []
    for c in range(8):
        b, q = c // 4, c % 4
        in_maps.append({
            "xT": xT_h[b],
            "xq": np.ascontiguousarray(xT_h[b][:, q * TQ:(q + 1) * TQ]),
            "wqkv": wqkv_h, "wproj": wproj_h, "bproj": bproj_h,
            "qscale": qs_h, "kscale": ks_h,
            "wgT": wgT_h, "wuT": wuT_h, "wdT": wdT_h,
        })
    res = run_bass_kernel_spmd(_NC, in_maps, core_ids=list(range(8)))
    outT = np.empty((B, C, N), dtype=np.float32)
    for c in range(8):
        b, q = c // 4, c % 4
        outT[b][:, q * TQ:(q + 1) * TQ] = np.asarray(res.results[c]["out"])
    return np.ascontiguousarray(outT.transpose(0, 2, 1))


# version 32
# speedup vs baseline: 1.0201x; 1.0201x over previous
import numpy as np
import concourse.bass as bass
import concourse.mybir as mybir
from concourse.bass_isa import ReduceOp
from concourse.bass_utils import run_bass_kernel_spmd

F32 = mybir.dt.float32
BF16 = mybir.dt.bfloat16
I32 = mybir.dt.int32
AF = mybir.ActivationFunctionType
ALU = mybir.AluOpType
AX = mybir.AxisListType

ENGINES = ("tensor", "scalar", "vector", "sync", "gpsimd")

B, N, C, H, HD = 2, 8192, 1024, 16, 64
TQ = 2048
NKT = N // 128
CC = C // 128
LN_EPS = 1e-5
RMS_EPS = 1e-8
DQ = 1.0 / 127.0
HID = 3072
HC = HID // 128

NONE_MARK = (None, 0)


class Sched:

    def __init__(self, nc):
        self.nc = nc
        self.stream = {e: [] for e in ENGINES}
        self.sem = {e: nc.alloc_semaphore("s_" + e) for e in ENGINES}
        self.s_out = nc.alloc_semaphore("s_out")
        self.cnt = {e: 0 for e in ENGINES}
        self.outcnt = 0
        self.lastwait = {e: {} for e in ENGINES}

    def op(self, e, fn, deps=(), dma=False, out=False):
        rec = [fn, [], []]
        for pe, v in deps:
            if pe is None or pe == e or v <= 0:
                continue
            if self.lastwait[e].get(pe, 0) >= v:
                continue
            self.lastwait[e][pe] = v
            rec[1].append((self.s_out if pe == "out" else self.sem[pe], v))
        if out:
            self.outcnt += 16
            rec[2].append((self.s_out, 16))
        elif dma:
            self.cnt[e] += 16
            rec[2].append((self.sem[e], 16))
        self.stream[e].append(rec)
        return rec

    def out_mark(self):
        return ("out", self.outcnt)

    def mark(self, e):
        for rec in reversed(self.stream[e]):
            if rec[0] is None:
                continue
            if not any(s is self.sem[e] for s, _ in rec[2]):
                self.cnt[e] += 1
                rec[2].append((self.sem[e], 1))
            return (e, self.cnt[e])
        return (e, 0)

    def wait_only(self, e, deps):
        rec = [None, [], []]
        for pe, v in deps:
            if pe is None or pe == e or v <= 0:
                continue
            if self.lastwait[e].get(pe, 0) >= v:
                continue
            self.lastwait[e][pe] = v
            rec[1].append((self.s_out if pe == "out" else self.sem[pe], v))
        if rec[1]:
            self.stream[e].append(rec)

    def barrier(self):
        ms = [self.mark(e) for e in ENGINES]
        for e in ENGINES:
            self.wait_only(e, ms)

    def final_out_wait(self, e):
        self.stream[e].append([None, [(self.s_out, self.outcnt)], []])

    def build_block(self):
        nc = self.nc
        with nc.Block() as block:
            def runner(ename):
                def run(eng):
                    for fn, waits, incs in self.stream[ename]:
                        for sem, v in waits:
                            eng.wait_ge(sem, v)
                        if fn is not None:
                            inst = fn(eng)
                            for sem, n in incs:
                                inst.then_inc(sem, n)
                return run
            block.tensor(runner("tensor"))
            block.scalar(runner("scalar"))
            block.vector(runner("vector"))
            block.sync(runner("sync"))
            block.gpsimd(runner("gpsimd"))


class _Trunc(Exception):
    pass


def build_program(stop_after=None):
    nc = bass.Bass("TRN2", target_bir_lowering=False)

    xT = nc.dram_tensor("xT", [C, N], F32, kind="ExternalInput")
    xq = nc.dram_tensor("xq", [C, TQ], F32, kind="ExternalInput")
    wqkv = nc.dram_tensor("wqkv", [C, 3 * C], F32, kind="ExternalInput")
    wproj = nc.dram_tensor("wproj", [C, C], F32, kind="ExternalInput")
    bproj = nc.dram_tensor("bproj", [C], F32, kind="ExternalInput")
    qscale = nc.dram_tensor("qscale", [HD], F32, kind="ExternalInput")
    kscale = nc.dram_tensor("kscale", [HD], F32, kind="ExternalInput")
    wgT = nc.dram_tensor("wgT", [C, HID], I32, kind="ExternalInput")
    wuT = nc.dram_tensor("wuT", [C, HID], I32, kind="ExternalInput")
    wdT = nc.dram_tensor("wdT", [HID, C], I32, kind="ExternalInput")
    out = nc.dram_tensor("out", [C, TQ], F32, kind="ExternalOutput")
    ksp = nc.dram_tensor("ksp", [8, 128, N], BF16, kind="Internal")
    vsp = nc.dram_tensor("vsp", [8, 128, NKT * 130], BF16, kind="Internal")
    qsp = nc.dram_tensor("qsp", [8, 128, TQ], BF16, kind="Internal")

    SB_BASE = 17408
    def sb(name, shape, dtype, off):
        return nc.alloc_sbuf_tensor_at(name, shape, dtype, offset=SB_BASE + off)

    ones_bf = sb("ones_bf", [128, 128], BF16, 0)
    bones_bf = sb("bones_bf", [128, 128], BF16, 256)
    onesf32 = sb("onesf32", [128, 128], F32, 512)
    qs8col = sb("qs8col", [128, 1], F32, 1024)
    kscol = sb("kscol", [128, 1], F32, 1056)
    negB = sb("negB", [128, 1], F32, 1088)
    maxQs2 = sb("maxQs2", [128, 1], F32, 1120)
    maxK2 = sb("maxK2", [128, 1], F32, 1152)
    tmpcol = sb("tmpcol", [128, 1], F32, 1184)
    bcol = sb("bcol", [128, 1], F32, 1216)
    bpcol = sb("bpcol", [128, 8], F32, 1248)
    brow = sb("brow", [128, 128], F32, 1280)
    epsln_col = sb("epsln_col", [128, 1], F32, 1792)
    negone_col = sb("negone_col", [128, 1], F32, 1824)
    bcol2 = sb("bcol2", [128, 1], F32, 1856)
    maxQs2b = sb("maxQs2b", [128, 1], F32, 1888)
    maxK2b = sb("maxK2b", [128, 1], F32, 1920)

    Qstg = [sb(f"qstg{i}", [128, 512], BF16, 2048 + i * 1024) for i in range(2)]
    FB = 133120
    wqkv_sb = [sb(f"wq{i}", [128, 3 * C], BF16, FB + i * 6144) for i in range(8)]
    Kstg = [sb(f"kst{i}", [128, 512], BF16, 203776 + i * 1024) for i in range(2)]
    Vstg = sb("vstg", [128, 8 * 260], BF16, 205824)

    xb_pp = [sb(f"xb{i}", [128, 4096], BF16, 34816 + i * 8192) for i in range(2)]
    xtmp_pp = [sb(f"xtp{i}", [128, 4096], BF16, 51200 + i * 8192) for i in range(2)]
    rscr = sb("rscr", [128, 512], F32, 100352)
    qsq_pp = [sb(f"qsqp{i}", [128, 512], BF16, 102400 + i * 1024) for i in range(2)]
    q2_pp = [sb(f"q2p{i}", [128, 512], BF16, 104448 + i * 1024) for i in range(2)]
    sd_pp = [sb(f"sdp{i}", [128, 512], F32, 106496 + i * 2048) for i in range(2)]
    rc_pp = [sb(f"rcp{i}", [128, 512], F32, 110592 + i * 2048) for i in range(2)]
    mrow_pp = [sb(f"mrw{i}", [128, 512], F32, 114688 + i * 2048) for i in range(2)]
    lnm2_pp = [sb(f"lm2{i}", [128, 512], F32, 118784 + i * 2048) for i in range(2)]
    lnvar_pp = [sb(f"lvr{i}", [128, 512], F32, 122880 + i * 2048) for i in range(2)]
    lnsd_pp = [sb(f"lsd{i}", [128, 512], F32, 126976 + i * 2048) for i in range(2)]
    rstdb_pp = [sb(f"rsb{i}", [128, 512], BF16, 131072 + i * 1024) for i in range(2)]

    Qsb = [sb(f"qsb{i}", [128, 512], BF16, 4096 + i * 1024) for i in range(2)]
    attnout2 = [[sb(f"ao{s}_{i}", [128, 512], BF16, 6144 + (s * 8 + i) * 1024)
                 for i in range(8)] for s in range(2)]
    delta1v = [sb(f"d1v{i}", [128, 512], BF16, 22528 + i * 1024) for i in range(8)]
    xn2v = [sb(f"xn2v{i}", [128, 512], BF16, 30720 + i * 1024) for i in range(8)]
    hact = [sb(f"ha{i}", [128, 512], BF16, 38912 + i * 1024) for i in range(24)]
    sq2 = sb("sq2", [128, 4096], BF16, 38912)
    r1b = sb("r1b", [128, 4096], BF16, 63488)
    xr = sb("xr", [128, 4096], F32, 71680)
    m2 = sb("m2", [128, 512], F32, 88064)
    mm2 = sb("mm2", [128, 512], F32, 90112)
    var2 = sb("var2", [128, 512], F32, 92160)
    sd2 = sb("sd2", [128, 512], F32, 94208)
    rstd2 = sb("rstd2", [128, 512], F32, 96256)
    wproj_sb = [sb(f"wp{i}", [128, C], BF16, 98304 + i * 2048) for i in range(8)]
    wgt = [sb(f"wgt{i}", [128, C], BF16, 114688 + i * 2048) for i in range(2)]
    wut = [sb(f"wut{i}", [128, C], BF16, 118784 + i * 2048) for i in range(2)]
    Pp = [sb("pph0", [128, 1024], BF16, 2048),
          sb("pph1", [128, 1024], BF16, 131072),
          sb("pph2", [128, 1024], BF16, 190528),
          sb("pph3", [128, 1024], BF16, 192576),
          sb("ppr0", [128, 1024], BF16, 122880),
          sb("ppr1", [128, 1024], BF16, 124928)]
    rrow = sb("rrow", [128, 512], F32, 126976)
    rrow2 = sb("rrow2", [128, 512], F32, 129024)
    Vhold = sb("vhold", [128, 520], BF16, 204928)
    rden = sb("rden", [128, 1024], F32, 194624)
    rtt = sb("rtt", [128, 512], F32, 206976)
    wd_sb = [sb(f"wd{i}", [128, C], BF16, FB + i * 2048) for i in range(24)]
    Ksb2 = [sb(f"ksb2{i}", [128, 1024], BF16, 182272 + i * 2048) for i in range(2)]
    Vsb2 = [sb(f"vsb2{i}", [128, 8 * 130], BF16, 186368 + i * 2080) for i in range(2)]
    gs = sb("gs", [128, 512], BF16, 198784)
    us = sb("us", [128, 512], BF16, 199808)
    tgs = sb("tgs", [128, 512], F32, 200832)
    trc = sb("trc", [128, 512], F32, 202880)

    ps_a = nc.alloc_psum_tensor("ps_a", [128, 512], F32)
    ps_b = nc.alloc_psum_tensor("ps_b", [128, 512], F32)
    ps_c = nc.alloc_psum_tensor("ps_c", [128, 512], F32)
    ps_d = nc.alloc_psum_tensor("ps_d", [128, 512], F32)
    ps_big = nc.alloc_psum_tensor("ps_big", [128, 2048], F32)

    S = Sched(nc)

    def lp(f):
        def g(e):
            with nc.allow_low_precision(reason="bf16 stream operand"):
                return f(e)
        return g

    def ckpt(name):
        if stop_after == name:
            raise _Trunc()

    try:
        S.op("vector", lambda e: e.memset(ones_bf[:, :], 1.0))
        S.op("vector", lambda e: e.memset(bones_bf[:, :], 0.0))
        S.op("vector", lambda e: e.memset(bones_bf[0:64, 0:64], 1.0))
        S.op("vector", lambda e: e.memset(bones_bf[64:128, 64:128], 1.0))
        S.op("vector", lambda e: e.memset(onesf32[:, :], 1.0))
        S.op("vector", lambda e: e.memset(
            Vstg[:].rearrange("p (n c) -> p n c", c=65)[:, :, 64:65], 1.0))
        S.op("vector", lambda e: e.memset(epsln_col[:, 0:1], LN_EPS))
        S.op("vector", lambda e: e.memset(negone_col[:, 0:1], -1.0))
        m_const = S.mark("vector")

        S.op("sync", lambda e: e.dma_start(
            out=qs8col[0:64, 0:1], in_=qscale[:].rearrange("(p o) -> p o", o=1)), dma=True)
        S.op("sync", lambda e: e.dma_start(
            out=qs8col[64:128, 0:1], in_=qscale[:].rearrange("(p o) -> p o", o=1)), dma=True)
        S.op("sync", lambda e: e.dma_start(
            out=kscol[0:64, 0:1], in_=kscale[:].rearrange("(p o) -> p o", o=1)), dma=True)
        S.op("sync", lambda e: e.dma_start(
            out=kscol[64:128, 0:1], in_=kscale[:].rearrange("(p o) -> p o", o=1)), dma=True)
        S.op("sync", lambda e: e.dma_start(
            out=bpcol[:, 0:8], in_=bproj[:].rearrange("(f p) -> p f", p=128),
            allow_slow_non_contiguous=True), dma=True)
        m_cols = S.mark("sync")
        S.op("vector", lambda e: e.tensor_scalar_mul(qs8col[:, 0:1], qs8col[:, 0:1], 0.125),
             deps=[m_cols])
        m_cols2 = S.mark("vector")

        S.op("vector", lambda e: e.scalar_tensor_tensor(
            bcol[:, 0:1], qs8col[:, 0:1], -1.0, qs8col[:, 0:1],
            op0=ALU.mult, op1=ALU.max), deps=[m_cols2])
        S.op("vector", lambda e: e.scalar_tensor_tensor(
            bcol2[:, 0:1], kscol[:, 0:1], -1.0, kscol[:, 0:1],
            op0=ALU.mult, op1=ALU.max), deps=[m_cols])
        m_abs = S.mark("vector")
        S.op("sync", lambda e: e.dma_start(out=brow[0:1, 0:128], in_=bcol[:, 0:1],
                                           allow_slow_non_contiguous=True),
             deps=[m_abs], dma=True)
        S.op("sync", lambda e: e.dma_start(out=brow[1:2, 0:128], in_=bcol2[:, 0:1],
                                           allow_slow_non_contiguous=True), dma=True)
        m_brow = S.mark("sync")
        S.op("vector", lambda e: e.reduce_max(tmpcol[0:2, 0:1], brow[0:2, 0:128],
                                              axis=AX.X), deps=[m_brow])
        m_red = S.mark("vector")
        S.op("sync", lambda e: e.dma_start(out=brow[0:1, 1:2], in_=tmpcol[1:2, 0:1]),
             deps=[m_red], dma=True)
        m_mv = S.mark("sync")
        S.op("vector", lambda e: e.tensor_mul(bcol2[0:1, 0:1], tmpcol[0:1, 0:1],
                                              brow[0:1, 1:2]), deps=[m_mv])
        S.op("vector", lambda e: e.tensor_scalar(
            bcol2[0:1, 0:1], bcol2[0:1, 0:1], -float(HD), -1.0,
            op0=ALU.mult, op1=ALU.add))
        m_b4 = S.mark("vector")
        S.op("tensor", lambda e: e.matmul(ps_a[:, 0:1], onesf32[0:1, 0:128],
                                          bcol2[0:1, 0:1], start=True, stop=True),
             deps=[m_b4, m_const])
        m_b5 = S.mark("tensor")
        S.op("vector", lambda e: e.tensor_copy(negB[:, 0:1], ps_a[:, 0:1]), deps=[m_b5])
        m_negB = S.mark("vector")
        stats_free = [S.mark("vector")]

        m_wq_cc = []
        for cc in range(CC):
            S.op("gpsimd", lambda e, cc=cc: e.dma_start(
                out=wqkv_sb[cc][:, :], in_=wqkv[cc * 128:(cc + 1) * 128, :]), dma=True)
            m_wq_cc.append(S.mark("gpsimd"))
        m_wqkv = m_wq_cc[-1]
        ckpt("P0")

        proj_banks = [ps_c[:, :], ps_big[:, 0:512], ps_big[:, 512:1024],
                      ps_big[:, 1024:1536]]
        proj_free = [[NONE_MARK] for _ in range(4)]
        bones_ms = ps_d[:, :]
        bones_bd = ps_big[:, 1536:2048]
        bones_ms_free = [NONE_MARK]
        bones_bd_free = [NONE_MARK]
        stats_free = [m_negB]
        rr = [0]
        ss = [0]
        slot_free = [{}, {}]
        rscr_rd = [NONE_MARK]

        def emit_ln(load_mark, xp):
            xb, xtmp = xb_pp[xp], xtmp_pp[xp]
            mrow, lnm2, lnvar = mrow_pp[xp], lnm2_pp[xp], lnvar_pp[xp]
            lnsd, rstd_b = lnsd_pp[xp], rstdb_pp[xp]
            S.op("scalar", lambda e: e.activation(xtmp[:, :], xb[:, :], AF.Square),
                 deps=[load_mark, m_const])
            m_sq = S.mark("scalar")
            for cc in range(CC):
                S.op("tensor", lambda e, cc=cc: e.matmul(
                    ps_a[:, :], ones_bf[:, :], xb[:, cc * 512:(cc + 1) * 512],
                    start=(cc == 0), stop=(cc == CC - 1)),
                    deps=[load_mark, m_const] + stats_free)
            for cc in range(CC):
                S.op("tensor", lambda e, cc=cc: e.matmul(
                    ps_b[:, :], ones_bf[:, :], xtmp[:, cc * 512:(cc + 1) * 512],
                    start=(cc == 0), stop=(cc == CC - 1)), deps=[m_sq])
            m_sums = S.mark("tensor")
            S.op("vector", lambda e: e.tensor_scalar_mul(mrow[:, :], ps_a[:, :], 1.0 / C),
                 deps=[m_sums])
            m_mrow = S.mark("vector")
            S.op("vector", lambda e: e.tensor_mul(lnm2[:, :], mrow[:, :], mrow[:, :]))
            S.op("vector", lambda e: e.scalar_tensor_tensor(
                lnvar[:, :], ps_b[:, :], 1.0 / C, lnm2[:, :],
                op0=ALU.mult, op1=ALU.subtract), deps=[m_sums])
            m_var = S.mark("vector")
            stats_free[:] = [m_var]
            S.op("scalar", lambda e: e.activation(
                lnsd[:, :], lnvar[:, :], AF.Sqrt, bias=epsln_col[:, 0:1]),
                deps=[m_var, m_const])
            m_sd = S.mark("scalar")
            S.op("vector", lp(lambda e: e.reciprocal(rstd_b[:, :], lnsd[:, :])),
                 deps=[m_sd])
            m_rstd = S.mark("vector")
            for cc in range(CC):
                sl = slice(cc * 512, (cc + 1) * 512)
                S.op("gpsimd", lambda e, sl=sl: e.tensor_sub(
                    xtmp[:, sl], xb[:, sl], mrow[:, :]), deps=[m_sums, m_mrow])
            m_cent = S.mark("gpsimd")
            for cc in range(CC):
                sl = slice(cc * 512, (cc + 1) * 512)
                S.op("vector", lambda e, sl=sl: e.tensor_mul(
                    xb[:, sl], xtmp[:, sl], rstd_b[:, :]),
                    deps=[m_cent, m_rstd])
            return S.mark("vector")

        def emit_rms(dst, scol, m_proj, pb, extra_out_deps=()):
            s = ss[0] % 2
            ss[0] += 1
            pp = proj_banks[pb]
            qsq, q2, sd, rc = qsq_pp[s], q2_pp[s], sd_pp[s], rc_pp[s]
            sf = slot_free[s]
            S.op("scalar", lambda e: e.activation(qsq[:, :], pp, AF.Square),
                 deps=[m_proj] + sf.get("qsq", []))
            m_cp = S.mark("scalar")
            S.op("tensor", lambda e: e.matmul(bones_ms, bones_bf[:, :], qsq[:, :],
                                              start=True, stop=True),
                 deps=[m_cp] + bones_ms_free)
            m_bo = S.mark("tensor")
            sf["qsq"] = [m_bo]
            S.op("scalar", lambda e: e.activation(
                sd[:, :], bones_ms, AF.Sqrt, scale=1.0 / HD),
                deps=[m_bo] + sf.get("sd", []))
            m_r = S.mark("scalar")
            bones_ms_free[:] = [m_r]
            S.op("vector", lambda e: e.tensor_scalar_add(rc[:, :], sd[:, :], RMS_EPS),
                 deps=[m_r])
            S.op("vector", lambda e: e.tensor_scalar_mul(sd[:, :], pp, scol[:, 0:1]),
                 deps=[m_proj])
            m_tmp = S.mark("vector")
            S.op("vector", lambda e: e.reciprocal(rscr[:, :], rc[:, :]))
            S.op("vector", lambda e, dst=dst: e.tensor_mul(dst, sd[:, :], rscr[:, :]),
                 deps=list(extra_out_deps))
            m_dst = S.mark("vector")
            sf["sd"] = [m_dst]
            proj_free[pb][:] = [m_cp, m_tmp]
            return m_dst

        xb_consumed = [[], []]
        pendq = None
        def emit_load(t0, src, xp, extra=()):
            S.op("gpsimd", lambda e, t0=t0, src=src, xp=xp: e.dma_start(
                out=xb_pp[xp][:].rearrange("p (cc t) -> p cc t", cc=CC),
                in_=src[:, t0:t0 + 512].rearrange("(cc p) t -> p cc t", p=128)),
                deps=xb_consumed[xp] + list(extra), dma=True)
            return S.mark("gpsimd")

        qstg_last = {}

        def flush_q(pq):
            pfc, pt0, ppar, pm_proj, ppb = pq
            extra = [qstg_last[ppar]] if ppar in qstg_last else []
            m_dst = emit_rms(Qstg[ppar][:, :], qs8col, pm_proj, ppb,
                             extra_out_deps=extra)
            S.op("sync", lambda e, pfc=pfc, pt0=pt0, ppar=ppar: e.dma_start(
                out=qsp[pfc, :, pt0:pt0 + 512], in_=Qstg[ppar][:, :]),
                deps=[m_dst], dma=True)
            qstg_last[ppar] = S.mark("sync")

        load_marks = {}
        ln_marks = {}
        load_marks[0] = emit_load(0, xq, 0)
        ln_marks[0] = emit_ln(load_marks[0], 0)
        for tc in range(TQ // 512):
            t0 = tc * 512
            xp = tc % 2
            if tc + 1 < TQ // 512:
                load_marks[tc + 1] = emit_load(t0 + 512, xq, (tc + 1) % 2)
                ln_marks[tc + 1] = emit_ln(load_marks[tc + 1], (tc + 1) % 2)
            m_xn = ln_marks[tc]
            xb = xb_pp[xp]
            for fc in range(CC):
                pb = rr[0] % 4
                rr[0] += 1
                for cc in range(CC):
                    S.op("tensor", lambda e, cc=cc, fc=fc, pb=pb, xb=xb: e.matmul(
                        proj_banks[pb], wqkv_sb[cc][:, fc * 128:(fc + 1) * 128],
                        xb[:, cc * 512:(cc + 1) * 512],
                        start=(cc == 0), stop=(cc == CC - 1)),
                        deps=[m_xn, m_wq_cc[cc]] + proj_free[pb])
                m_proj = S.mark("tensor")
                if pendq is not None:
                    flush_q(pendq)
                pendq = (fc, t0, fc % 2, m_proj, pb)
            xb_consumed[xp] = [S.mark("tensor"), S.mark("vector"), S.mark("gpsimd"),
                               S.mark("scalar")]
        flush_q(pendq)
        pendq = None

        ckpt("P1")

        kstg_last = {}
        vstg_last = {}
        load_marks = {}
        ln_marks = {}
        load_marks[0] = emit_load(0, xT, 0, extra=xb_consumed[0])
        ln_marks[0] = emit_ln(load_marks[0], 0)
        for tc in range(N // 512):
            t0 = tc * 512
            xp = tc % 2
            if tc + 1 < N // 512:
                ex = xb_consumed[(tc + 1) % 2] if tc == 0 else ()
                load_marks[tc + 1] = emit_load(t0 + 512, xT, (tc + 1) % 2, extra=ex)
                ln_marks[tc + 1] = emit_ln(load_marks[tc + 1], (tc + 1) % 2)
            m_xn = ln_marks[tc]
            xb = xb_pp[xp]
            pendk = None

            def flush_k(pk):
                pfc, pt0, ppar, pm_proj, ppb = pk
                extra = [kstg_last[ppar]] if ppar in kstg_last else []
                m_dst = emit_rms(Kstg[ppar][:, :], kscol, pm_proj, ppb,
                                 extra_out_deps=extra)
                S.op("sync", lambda e, pfc=pfc, pt0=pt0, ppar=ppar: e.dma_start(
                    out=ksp[pfc, :, pt0:pt0 + 512], in_=Kstg[ppar][:, :]),
                    deps=[m_dst], dma=True)
                kstg_last[ppar] = S.mark("sync")

            for fc in range(CC):
                pb = rr[0] % 4
                rr[0] += 1
                for cc in range(CC):
                    S.op("tensor", lambda e, cc=cc, fc=fc, pb=pb, xb=xb: e.matmul(
                        proj_banks[pb], wqkv_sb[cc][:, C + fc * 128: C + (fc + 1) * 128],
                        xb[:, cc * 512:(cc + 1) * 512],
                        start=(cc == 0), stop=(cc == CC - 1)),
                        deps=[m_xn] + proj_free[pb])
                m_proj = S.mark("tensor")
                if pendk is not None:
                    flush_k(pendk)
                pendk = (fc, t0, fc % 2, m_proj, pb)
            flush_k(pendk)
            for vg in range(2):
                for tt in range(4):
                    kt = tc * 4 + tt
                    c0 = 2 * C + vg * 512
                    pb = rr[0] % 4
                    rr[0] += 1
                    for cc in range(CC):
                        S.op("tensor", lambda e, cc=cc, c0=c0, tt=tt, pb=pb, xb=xb: e.matmul(
                            proj_banks[pb],
                            xb[:, cc * 512 + tt * 128: cc * 512 + tt * 128 + 128],
                            wqkv_sb[cc][:, c0:c0 + 512],
                            start=(cc == 0), stop=(cc == CC - 1)),
                            deps=[m_xn] + proj_free[pb])
                    m_v = S.mark("tensor")
                    parity = (tt // 2) % 2
                    half = tt % 2
                    dep_sp = [m for r in range(4)
                              for m in ([vstg_last[(r, parity)]]
                                        if (r, parity) in vstg_last else [])]
                    out5 = Vstg[:, parity * 1040:(parity + 1) * 1040].rearrange(
                        "p (r hf h c) -> p r hf h c", r=4, hf=2, h=2)[
                        :, :, half:half + 1, :, 0:64]
                    S.op("scalar", lambda e, out5=out5, pb=pb: e.copy(
                        out=out5,
                        in_=proj_banks[pb].rearrange(
                            "p (r o h c) -> p r o h c", r=4, o=1, h=2)),
                        deps=[m_v, m_const] + dep_sp)
                    m_cp = S.mark("scalar")
                    proj_free[pb][:] = [m_cp]
                    if half == 1:
                        for r in range(4):
                            stg0 = parity * 4 + r
                            reg = 4 * vg + r
                            S.op("sync", lambda e, reg=reg, kt=kt, stg0=stg0: e.dma_start(
                                out=vsp[reg, :, (kt - 1) * 130:(kt + 1) * 130],
                                in_=Vstg[:, stg0 * 260: stg0 * 260 + 260]),
                                deps=[m_cp], dma=True)
                            vstg_last[(r, parity)] = S.mark("sync")
            xb_consumed[xp] = [S.mark("tensor"), S.mark("vector"), S.mark("gpsimd"),
                               S.mark("scalar")]

        ckpt("P2")
        m_kspill = S.mark("sync")
        m_vspill = S.mark("sync")

        ckpt("B")
        S.barrier()

        for cc in range(CC):
            S.op("gpsimd", lambda e, cc=cc: e.dma_start(
                out=wproj_sb[cc][:, :], in_=wproj[cc * 128:(cc + 1) * 128, :]), dma=True)
        m_wproj = S.mark("gpsimd")
        for i in range(24):
            S.op("gpsimd", lambda e, i=i: e.dma_start(
                out=wd_sb[i][:, :], in_=wdT[i * 128:(i + 1) * 128, :]), dma=True)
        m_wd = S.mark("gpsimd")

        tbank = [ps_c, ps_d]
        tb_free = [[NONE_MARK], [NONE_MARK]]
        ao_done = {}
        ao_free = {0: [NONE_MARK], 1: [NONE_MARK]}
        d1_free = [NONE_MARK]
        xr_free = [NONE_MARK]
        xn_free = [NONE_MARK]
        hact_free = [NONE_MARK]
        r1b_free = [NONE_MARK]
        m_tgs_rd = [NONE_MARK]
        m_us_rd = [NONE_MARK]
        wg_last = {}

        def tail_gen(qcp):
            slot = qcp % 2
            t0 = qcp * 512
            S.op("sync", lambda e, t0=t0: e.dma_start(
                out=xr[:].rearrange("p (cc t) -> p cc t", cc=CC),
                in_=xq[:, t0:t0 + 512].rearrange("(cc p) t -> p cc t", p=128)),
                deps=[xr_free[0]], dma=True)
            m_xr = S.mark("sync")
            yield
            d1w = d1_free[0]
            for fo in range(CC):
                bk = fo % 2
                for cc in range(CC):
                    S.op("tensor", lambda e, cc=cc, fo=fo, bk=bk, slot=slot: e.matmul(
                        tbank[bk][:, :], wproj_sb[cc][:, fo * 128:(fo + 1) * 128],
                        attnout2[slot][cc][:, :],
                        start=(cc == 0), stop=(cc == CC - 1)),
                        deps=[m_wproj, ao_done[qcp]] + tb_free[bk])
                    if cc % 2 == 1:
                        yield
                m_p = S.mark("tensor")
                S.op("vector", lambda e, fo=fo, bk=bk: e.tensor_scalar_add(
                    delta1v[fo][:, :], tbank[bk][:, :], bpcol[:, fo:fo + 1]),
                    deps=[m_p, m_cols, d1w])
                tb_free[bk][:] = [S.mark("vector")]
            ao_free[slot][:] = [S.mark("tensor")]
            m_d1 = S.mark("vector")
            for cc in range(CC):
                sl = slice(cc * 512, (cc + 1) * 512)
                S.op("vector", lambda e, cc=cc, sl=sl: e.tensor_add(
                    r1b[:, sl], xr[:, sl], delta1v[cc][:, :]),
                    deps=[m_xr, m_d1, r1b_free[0]])
            m_r1 = S.mark("vector")
            for cc in range(CC):
                S.op("tensor", lambda e, cc=cc: e.matmul(
                    ps_c[:, :], ones_bf[:, :], r1b[:, cc * 512:(cc + 1) * 512],
                    start=(cc == 0), stop=(cc == CC - 1)),
                    deps=[m_r1] + tb_free[0])
                if cc % 2 == 1:
                    yield
            m_sum1 = S.mark("tensor")
            S.op("vector", lambda e: e.tensor_mul(sq2[:, :], r1b[:, :], r1b[:, :]),
                 deps=[m_r1, hact_free[0]])
            m_sq = S.mark("vector")
            yield
            for cc in range(CC):
                S.op("tensor", lambda e, cc=cc: e.matmul(
                    ps_d[:, :], ones_bf[:, :], sq2[:, cc * 512:(cc + 1) * 512],
                    start=(cc == 0), stop=(cc == CC - 1)),
                    deps=[m_sq] + tb_free[1])
                if cc % 2 == 1:
                    yield
            m_sums = S.mark("tensor")
            S.op("vector", lambda e: e.tensor_scalar_mul(m2[:, :], ps_c[:, :], 1.0 / C),
                 deps=[m_sums, m_sum1])
            S.op("vector", lambda e: e.tensor_mul(mm2[:, :], m2[:, :], m2[:, :]))
            S.op("vector", lambda e: e.scalar_tensor_tensor(
                var2[:, :], ps_d[:, :], 1.0 / C, mm2[:, :],
                op0=ALU.mult, op1=ALU.subtract), deps=[m_sums])
            m_var = S.mark("vector")
            tb_free[0][:] = [m_var]
            tb_free[1][:] = [m_var]
            S.op("scalar", lambda e: e.activation(sd2[:, :], var2[:, :], AF.Sqrt,
                                                  bias=epsln_col[:, 0:1]), deps=[m_var])
            m_sd = S.mark("scalar")
            S.op("vector", lambda e: e.reciprocal(rstd2[:, :], sd2[:, :]), deps=[m_sd])
            S.op("vector", lambda e: e.tensor_scalar_mul(rstd2[:, :], rstd2[:, :], DQ))
            m_rstd = S.mark("vector")
            xnw = xn_free[0]
            for cc in range(CC):
                sl = slice(cc * 512, (cc + 1) * 512)
                S.op("vector", lambda e, sl=sl: e.tensor_sub(
                    sq2[:, sl], r1b[:, sl], m2[:, :]), deps=[m_rstd])
                S.op("vector", lambda e, cc=cc, sl=sl: e.tensor_mul(
                    xn2v[cc][:, :], sq2[:, sl], rstd2[:, :]), deps=[xnw])
            m_xn2 = S.mark("vector")
            r1b_free[0] = m_xn2
            yield
            hact_w = [NONE_MARK] * HC
            for hc in range(HC):
                buf = hc % 2
                ld = [wg_last[buf]] if buf in wg_last else []
                S.op("gpsimd", lambda e, hc=hc, buf=buf: e.dma_start(
                    out=wgt[buf][:].rearrange("p (cc h) -> p cc h", cc=CC),
                    in_=wgT[:, hc * 128:(hc + 1) * 128].rearrange(
                        "(cc p) h -> p cc h", p=128)), deps=ld, dma=True)
                S.op("gpsimd", lambda e, hc=hc, buf=buf: e.dma_start(
                    out=wut[buf][:].rearrange("p (cc h) -> p cc h", cc=CC),
                    in_=wuT[:, hc * 128:(hc + 1) * 128].rearrange(
                        "(cc p) h -> p cc h", p=128)), dma=True)
                m_w = S.mark("gpsimd")
                for cc in range(CC):
                    S.op("tensor", lambda e, cc=cc, buf=buf: e.matmul(
                        ps_c[:, :], wgt[buf][:, cc * 128:(cc + 1) * 128],
                        xn2v[cc][:, :], start=(cc == 0), stop=(cc == CC - 1)),
                        deps=[m_w, m_xn2] + tb_free[0])
                    if cc % 2 == 1:
                        yield
                m_g = S.mark("tensor")
                for cc in range(CC):
                    S.op("tensor", lambda e, cc=cc, buf=buf: e.matmul(
                        ps_d[:, :], wut[buf][:, cc * 128:(cc + 1) * 128],
                        xn2v[cc][:, :], start=(cc == 0), stop=(cc == CC - 1)),
                        deps=[m_w] + tb_free[1])
                    if cc % 2 == 1:
                        yield
                m_u = S.mark("tensor")
                wg_last[buf] = m_u
                S.op("vector", lp(lambda e: e.tensor_copy(gs[:, :], ps_c[:, :])),
                     deps=[m_g, m_us_rd[0]])
                m_gc = S.mark("vector")
                tb_free[0][:] = [m_gc]
                S.op("vector", lp(lambda e: e.tensor_scalar_mul(
                    us[:, :], ps_d[:, :], DQ)), deps=[m_u])
                tb_free[1][:] = [S.mark("vector")]
                S.op("scalar", lambda e: e.activation(tgs[:, :], gs[:, :], AF.Exp,
                                                      scale=-1.0),
                     deps=[m_gc, m_tgs_rd[0]])
                m_e = S.mark("scalar")
                S.op("vector", lambda e: e.tensor_scalar_add(tgs[:, :], tgs[:, :], 1.0),
                     deps=[m_e])
                S.op("vector", lambda e: e.reciprocal(trc[:, :], tgs[:, :]))
                S.op("vector", lambda e: e.tensor_mul(tgs[:, :], gs[:, :], trc[:, :]))
                S.op("vector", lambda e, hc=hc: e.tensor_mul(
                    hact[hc][:, :], tgs[:, :], us[:, :]),
                    deps=[hact_free[0]])
                hact_w[hc] = S.mark("vector")
                m_tgs_rd[0] = hact_w[hc]
                m_us_rd[0] = m_e
            for dc in range(CC):
                bk = dc % 2
                for hc in range(HC):
                    S.op("tensor", lambda e, hc=hc, dc=dc, bk=bk: e.matmul(
                        tbank[bk][:, :], wd_sb[hc][:, dc * 128:(dc + 1) * 128],
                        hact[hc][:, :], start=(hc == 0), stop=(hc == HC - 1)),
                        deps=[m_wd, hact_w[hc]] + tb_free[bk])
                    if hc % 2 == 1:
                        yield
                m_dn = S.mark("tensor")
                S.op("vector", lambda e, dc=dc: e.tensor_add(
                    m2[:, :], xr[:, dc * 512:(dc + 1) * 512], delta1v[dc][:, :]),
                    deps=[m_dn])
                S.op("vector", lambda e, bk=bk: e.tensor_add(
                    mm2[:, :], tbank[bk][:, :], m2[:, :]))
                m_o = S.mark("vector")
                tb_free[bk][:] = [m_o]
                S.op("sync", lambda e, dc=dc, t0=t0: e.dma_start(
                    out=out[dc * 128:(dc + 1) * 128, t0:t0 + 512], in_=mm2[:, :]),
                    deps=[m_o], out=True)
                S.wait_only("vector", [S.out_mark()])
            m_te_v = S.mark("vector")
            m_te_t = S.mark("tensor")
            d1_free[0] = m_te_v
            xr_free[0] = m_te_v
            xn_free[0] = m_te_t
            hact_free[0] = m_te_t

        def drain(g, n):
            if g is None:
                return None
            for _ in range(n):
                try:
                    next(g)
                except StopIteration:
                    return None
            return g

        NCK = 8
        CHUNKS = [(qcx, fcx, ckx) for qcx in range(4) for fcx in range(8)
                  for ckx in range(NCK)]
        kfree = [[NONE_MARK], [NONE_MARK]]
        vfree = [[NONE_MARK], [NONE_MARK]]
        qsb_free = [[NONE_MARK], [NONE_MARK]]
        loadmark = {}

        def issue_load(g):
            if g >= len(CHUNKS) or g in loadmark:
                return
            _, fcx, ckx = CHUNKS[g]
            b = g % 2
            S.op("sync", lambda e, fcx=fcx, ckx=ckx, b=b: e.dma_start(
                out=Ksb2[b][:, :], in_=ksp[fcx, :, ckx * 1024:(ckx + 1) * 1024]),
                deps=[m_kspill] + kfree[b], dma=True)
            S.op("sync", lambda e, fcx=fcx, ckx=ckx, b=b: e.dma_start(
                out=Vsb2[b][:, :], in_=vsp[fcx, :, ckx * 1040:(ckx + 1) * 1040]),
                deps=[m_vspill] + vfree[b], dma=True)
            loadmark[g] = S.mark("sync")

        exp_mark = {0: NONE_MARK, 1: NONE_MARK}
        pp_free = {i: NONE_MARK for i in range(6)}
        evac_ref = [NONE_MARK]
        stash = []
        vhold_free = [NONE_MARK]
        issue_load(0)
        issue_load(1)

        def pslot(kt):
            return kt if kt < 4 else 4 + (kt % 2)

        gen = None
        for qc in range(4):
            slot = qc % 2
            if qc > 0:
                gen = tail_gen(qc - 1)
            for fc in range(8):
                g0 = (qc * 8 + fc) * NCK
                qb = fc % 2
                S.op("sync", lambda e, fc=fc, qc=qc, qb=qb: e.dma_start(
                    out=Qsb[qb][:, :], in_=qsp[fc, :, qc * 512:(qc + 1) * 512]),
                    deps=[m_kspill] + qsb_free[qb], dma=True)
                S.op("sync", lambda e, fc=fc: e.dma_start(
                    out=Vhold[:, :], in_=vsp[fc, :, 0:520]),
                    deps=[m_vspill, vhold_free[0]], dma=True)
                m_q = S.mark("sync")
                exps = [NONE_MARK] * NKT

                def emit_av(ktav, deps_extra=()):
                    ps = pslot(ktav)
                    bv = (g0 + ktav // NCK) % 2
                    vo = (ktav % NCK) * 130
                    vsrc = Vhold if ktav < 4 else Vsb2[bv]
                    if ktav < 4:
                        vo = ktav * 130
                    S.op("tensor", lambda e, vsrc=vsrc, vo=vo, ktav=ktav, ps=ps: e.matmul(
                        ps_a[0:65, :], vsrc[:, vo:vo + 65],
                        Pp[ps][:, 0:512], start=(ktav == 4), stop=(ktav == 3)),
                        deps=[exps[ktav], loadmark[g0 + ktav // NCK]]
                        + list(deps_extra))
                    S.op("tensor", lambda e, vsrc=vsrc, vo=vo, ktav=ktav, ps=ps: e.matmul(
                        ps_b[0:65, :], vsrc[:, vo + 65:vo + 130],
                        Pp[ps][:, 512:1024], start=(ktav == 4), stop=(ktav == 3)))
                    m = S.mark("tensor")
                    pp_free[ps] = m
                    if ktav % NCK == NCK - 1:
                        vfree[bv][:] = [m]
                    return m

                for kt in range(NKT):
                    ck = kt // NCK
                    b = (g0 + ck) % 2
                    half = kt % 2
                    hb = half * 1024
                    ko = (kt % NCK) * 128
                    S.op("tensor", lambda e, b=b, ko=ko, hb=hb, qb=qb: e.matmul(
                        ps_big[:, hb:hb + 512],
                        Ksb2[b][0:64, ko:ko + 128],
                        Qsb[qb][0:64, :], start=True, stop=True),
                        deps=[m_q, loadmark[g0 + ck], exp_mark[half]])
                    S.op("tensor", lambda e, b=b, ko=ko, hb=hb, qb=qb: e.matmul(
                        ps_big[:, hb + 512:hb + 1024],
                        Ksb2[b][64:128, ko:ko + 128],
                        Qsb[qb][64:128, :], start=True, stop=True))
                    m_st = S.mark("tensor")
                    if kt % NCK == NCK - 1:
                        kfree[b][:] = [m_st]
                    if kt in (4, 5) and stash:
                        stash.pop(0)()
                    if kt >= 5:
                        emit_av(kt - 1,
                                deps_extra=([evac_ref[0]] if kt - 1 == 4 else ()))
                    if kt % NCK == 0:
                        issue_load(g0 + ck + 1)
                    gen = drain(gen, 1)
                    S.op("scalar", lambda e, ps=pslot(kt), hb=hb: e.activation(
                        Pp[ps][:, :], ps_big[:, hb:hb + 1024], AF.Exp,
                        bias=negB[:, 0:1]), deps=[m_st, pp_free[pslot(kt)]])
                    exp_mark[half] = S.mark("scalar")
                    exps[kt] = exp_mark[half]
                emit_av(NKT - 1)
                emit_av(0)
                emit_av(1)
                emit_av(2)
                m_av = emit_av(3)
                m_vh = S.mark("tensor")
                vhold_free[0] = m_vh
                qsb_free[qb][:] = [m_vh]
                S.op("vector", lambda e: e.tensor_copy(
                    rrow[64:65, :], ps_a[64:65, :]), deps=[m_av])
                S.op("vector", lambda e: e.tensor_copy(
                    rrow2[64:65, :], ps_b[64:65, :]))
                for u in range(2):
                    bank = (ps_a, ps_b)[u]
                    rbuf = (rrow, rrow2)[u]
                    rs = slice(u * 512, u * 512 + 512)
                    S.op("vector", lambda e, rbuf=rbuf, rs=rs: e.reciprocal(
                        rden[64:65, rs], rbuf[64:65, :]))
                    S.op("vector", lambda e, bank=bank, rbuf=rbuf: e.tensor_copy(
                        rbuf[0:64, :], bank[0:64, :]))
                    m_rc = S.mark("vector")

                    def mk_bc(u=u, bank=bank, rbuf=rbuf, rs=rs, m_rc=m_rc, fc=fc,
                              slot=slot, aof=list(ao_free[slot])):
                        def emit():
                            S.op("tensor", lambda e, rs=rs: e.matmul(
                                bank[0:64, :], onesf32[64:65, 0:64],
                                rden[64:65, rs], start=True, stop=True),
                                deps=[m_rc])
                            m_bc = S.mark("tensor")
                            S.op("vector", lambda e: e.tensor_mul(
                                attnout2[slot][fc][u * 64:(u + 1) * 64, :],
                                rbuf[0:64, :], bank[0:64, :]),
                                deps=[m_bc] + aof)
                            evac_ref[0] = S.mark("vector")
                        return emit
                    stash.append(mk_bc())
            while stash:
                stash.pop(0)()
            ao_done[qc] = S.mark("vector")
            ckpt("P3q%d" % qc)
        while gen is not None:
            gen = drain(gen, 16)
        gen = tail_gen(3)
        while gen is not None:
            gen = drain(gen, 16)
        ckpt("P6")


    except _Trunc:
        pass
    S.final_out_wait("sync")
    S.build_block()
    return nc


_NC = None


def kernel(L0_attn_x, L0_attn_Wqkv, L0_attn_Wproj, L0_attn_bproj,
           L0_attn_q_scale, L0_attn_k_scale,
           L0_ffn_w_gate, L0_ffn_w_up, L0_ffn_w_down):
    global _NC
    x = np.asarray(L0_attn_x, dtype=np.float32)
    wqkv_h = np.ascontiguousarray(np.asarray(L0_attn_Wqkv, dtype=np.float32))
    wproj_h = np.ascontiguousarray(np.asarray(L0_attn_Wproj, dtype=np.float32))
    bproj_h = np.ascontiguousarray(np.asarray(L0_attn_bproj, dtype=np.float32))
    qs_h = np.ascontiguousarray(np.asarray(L0_attn_q_scale, dtype=np.float32))
    ks_h = np.ascontiguousarray(np.asarray(L0_attn_k_scale, dtype=np.float32))
    wgT_h = np.ascontiguousarray(np.asarray(L0_ffn_w_gate, dtype=np.int32).T)
    wuT_h = np.ascontiguousarray(np.asarray(L0_ffn_w_up, dtype=np.int32).T)
    wdT_h = np.ascontiguousarray(np.asarray(L0_ffn_w_down, dtype=np.int32).T)
    xT_h = np.ascontiguousarray(x.transpose(0, 2, 1))

    if _NC is None:
        _NC = build_program()

    in_maps = []
    for c in range(8):
        b, q = c // 4, c % 4
        in_maps.append({
            "xT": xT_h[b],
            "xq": np.ascontiguousarray(xT_h[b][:, q * TQ:(q + 1) * TQ]),
            "wqkv": wqkv_h, "wproj": wproj_h, "bproj": bproj_h,
            "qscale": qs_h, "kscale": ks_h,
            "wgT": wgT_h, "wuT": wuT_h, "wdT": wdT_h,
        })
    res = run_bass_kernel_spmd(_NC, in_maps, core_ids=list(range(8)))
    outT = np.empty((B, C, N), dtype=np.float32)
    for c in range(8):
        b, q = c // 4, c % 4
        outT[b][:, q * TQ:(q + 1) * TQ] = np.asarray(res.results[c]["out"])
    return np.ascontiguousarray(outT.transpose(0, 2, 1))


# version 33 (final)
# speedup vs baseline: 1.0307x; 1.0104x over previous
import numpy as np
import concourse.bass as bass
import concourse.mybir as mybir
from concourse.bass_isa import ReduceOp
from concourse.bass_utils import run_bass_kernel_spmd

F32 = mybir.dt.float32
BF16 = mybir.dt.bfloat16
I32 = mybir.dt.int32
AF = mybir.ActivationFunctionType
ALU = mybir.AluOpType
AX = mybir.AxisListType

ENGINES = ("tensor", "scalar", "vector", "sync", "gpsimd")

B, N, C, H, HD = 2, 8192, 1024, 16, 64
TQ = 2048
NKT = N // 128
CC = C // 128
LN_EPS = 1e-5
RMS_EPS = 1e-8
DQ = 1.0 / 127.0
HID = 3072
HC = HID // 128

NONE_MARK = (None, 0)


class Sched:

    def __init__(self, nc):
        self.nc = nc
        self.stream = {e: [] for e in ENGINES}
        self.sem = {e: nc.alloc_semaphore("s_" + e) for e in ENGINES}
        self.s_out = nc.alloc_semaphore("s_out")
        self.cnt = {e: 0 for e in ENGINES}
        self.outcnt = 0
        self.lastwait = {e: {} for e in ENGINES}

    def op(self, e, fn, deps=(), dma=False, out=False):
        rec = [fn, [], []]
        for pe, v in deps:
            if pe is None or pe == e or v <= 0:
                continue
            if self.lastwait[e].get(pe, 0) >= v:
                continue
            self.lastwait[e][pe] = v
            rec[1].append((self.s_out if pe == "out" else self.sem[pe], v))
        if out:
            self.outcnt += 16
            rec[2].append((self.s_out, 16))
        elif dma:
            self.cnt[e] += 16
            rec[2].append((self.sem[e], 16))
        self.stream[e].append(rec)
        return rec

    def out_mark(self):
        return ("out", self.outcnt)

    def mark(self, e):
        for rec in reversed(self.stream[e]):
            if rec[0] is None:
                continue
            if not any(s is self.sem[e] for s, _ in rec[2]):
                self.cnt[e] += 1
                rec[2].append((self.sem[e], 1))
            return (e, self.cnt[e])
        return (e, 0)

    def wait_only(self, e, deps):
        rec = [None, [], []]
        for pe, v in deps:
            if pe is None or pe == e or v <= 0:
                continue
            if self.lastwait[e].get(pe, 0) >= v:
                continue
            self.lastwait[e][pe] = v
            rec[1].append((self.s_out if pe == "out" else self.sem[pe], v))
        if rec[1]:
            self.stream[e].append(rec)

    def barrier(self):
        ms = [self.mark(e) for e in ENGINES]
        for e in ENGINES:
            self.wait_only(e, ms)

    def final_out_wait(self, e):
        self.stream[e].append([None, [(self.s_out, self.outcnt)], []])

    def build_block(self):
        nc = self.nc
        with nc.Block() as block:
            def runner(ename):
                def run(eng):
                    for fn, waits, incs in self.stream[ename]:
                        for sem, v in waits:
                            eng.wait_ge(sem, v)
                        if fn is not None:
                            inst = fn(eng)
                            for sem, n in incs:
                                inst.then_inc(sem, n)
                return run
            block.tensor(runner("tensor"))
            block.scalar(runner("scalar"))
            block.vector(runner("vector"))
            block.sync(runner("sync"))
            block.gpsimd(runner("gpsimd"))


class _Trunc(Exception):
    pass


def build_program(stop_after=None):
    nc = bass.Bass("TRN2", target_bir_lowering=False)

    xT = nc.dram_tensor("xT", [C, N], F32, kind="ExternalInput")
    xq = nc.dram_tensor("xq", [C, TQ], F32, kind="ExternalInput")
    wqkv = nc.dram_tensor("wqkv", [C, 3 * C], F32, kind="ExternalInput")
    wproj = nc.dram_tensor("wproj", [C, C], F32, kind="ExternalInput")
    bproj = nc.dram_tensor("bproj", [C], F32, kind="ExternalInput")
    qscale = nc.dram_tensor("qscale", [HD], F32, kind="ExternalInput")
    kscale = nc.dram_tensor("kscale", [HD], F32, kind="ExternalInput")
    wgT = nc.dram_tensor("wgT", [C, HID], I32, kind="ExternalInput")
    wuT = nc.dram_tensor("wuT", [C, HID], I32, kind="ExternalInput")
    wdT = nc.dram_tensor("wdT", [HID, C], I32, kind="ExternalInput")
    out = nc.dram_tensor("out", [C, TQ], F32, kind="ExternalOutput")
    ksp = nc.dram_tensor("ksp", [8, 128, N], BF16, kind="Internal")
    vsp = nc.dram_tensor("vsp", [8, 128, NKT * 130], BF16, kind="Internal")
    qsp = nc.dram_tensor("qsp", [8, 128, TQ], BF16, kind="Internal")

    SB_BASE = 17408
    def sb(name, shape, dtype, off):
        return nc.alloc_sbuf_tensor_at(name, shape, dtype, offset=SB_BASE + off)

    ones_bf = sb("ones_bf", [128, 128], BF16, 0)
    bones_bf = sb("bones_bf", [128, 128], BF16, 256)
    onesf32 = sb("onesf32", [128, 128], F32, 512)
    qs8col = sb("qs8col", [128, 1], F32, 1024)
    kscol = sb("kscol", [128, 1], F32, 1056)
    negB = sb("negB", [128, 1], F32, 1088)
    maxQs2 = sb("maxQs2", [128, 1], F32, 1120)
    maxK2 = sb("maxK2", [128, 1], F32, 1152)
    tmpcol = sb("tmpcol", [128, 1], F32, 1184)
    bcol = sb("bcol", [128, 1], F32, 1216)
    bpcol = sb("bpcol", [128, 8], F32, 1248)
    brow = sb("brow", [128, 128], F32, 1280)
    epsln_col = sb("epsln_col", [128, 1], F32, 1792)
    negone_col = sb("negone_col", [128, 1], F32, 1824)
    bcol2 = sb("bcol2", [128, 1], F32, 1856)
    maxQs2b = sb("maxQs2b", [128, 1], F32, 1888)
    maxK2b = sb("maxK2b", [128, 1], F32, 1920)

    Qstg = [sb(f"qstg{i}", [128, 512], BF16, 2048 + i * 1024) for i in range(2)]
    FB = 133120
    wqkv_sb = [sb(f"wq{i}", [128, 3 * C], BF16, FB + i * 6144) for i in range(8)]
    Kstg = [sb(f"kst{i}", [128, 512], BF16, 203776 + i * 1024) for i in range(2)]
    Vstg = sb("vstg", [128, 8 * 260], BF16, 205824)

    xb_pp = [sb(f"xb{i}", [128, 4096], BF16, 34816 + i * 8192) for i in range(2)]
    xtmp_pp = [sb(f"xtp{i}", [128, 4096], BF16, 51200 + i * 8192) for i in range(2)]
    rscr = sb("rscr", [128, 512], F32, 100352)
    qsq_pp = [sb(f"qsqp{i}", [128, 512], BF16, 102400 + i * 1024) for i in range(2)]
    q2_pp = [sb(f"q2p{i}", [128, 512], BF16, 104448 + i * 1024) for i in range(2)]
    sd_pp = [sb(f"sdp{i}", [128, 512], F32, 106496 + i * 2048) for i in range(2)]
    rc_pp = [sb(f"rcp{i}", [128, 512], F32, 110592 + i * 2048) for i in range(2)]
    mrow_pp = [sb(f"mrw{i}", [128, 512], F32, 114688 + i * 2048) for i in range(2)]
    lnm2_pp = [sb(f"lm2{i}", [128, 512], F32, 118784 + i * 2048) for i in range(2)]
    lnvar_pp = [sb(f"lvr{i}", [128, 512], F32, 122880 + i * 2048) for i in range(2)]
    lnsd_pp = [sb(f"lsd{i}", [128, 512], F32, 126976 + i * 2048) for i in range(2)]
    rstdb_pp = [sb(f"rsb{i}", [128, 512], BF16, 131072 + i * 1024) for i in range(2)]

    Qsb = [sb(f"qsb{i}", [128, 512], BF16, 4096 + i * 1024) for i in range(2)]
    attnout2 = [[sb(f"ao{s}_{i}", [128, 512], BF16, 6144 + (s * 8 + i) * 1024)
                 for i in range(8)] for s in range(2)]
    delta1v = [sb(f"d1v{i}", [128, 512], BF16, 22528 + i * 1024) for i in range(8)]
    xn2v = [sb(f"xn2v{i}", [128, 512], BF16, 30720 + i * 1024) for i in range(8)]
    hact = [sb(f"ha{i}", [128, 512], BF16, 38912 + i * 1024) for i in range(24)]
    sq2 = sb("sq2", [128, 4096], BF16, 38912)
    r1b = sb("r1b", [128, 4096], BF16, 63488)
    xr = sb("xr", [128, 4096], F32, 71680)
    m2 = sb("m2", [128, 512], F32, 88064)
    mm2 = sb("mm2", [128, 512], F32, 90112)
    var2 = sb("var2", [128, 512], F32, 92160)
    sd2 = sb("sd2", [128, 512], F32, 94208)
    rstd2 = sb("rstd2", [128, 512], F32, 96256)
    wproj_sb = [sb(f"wp{i}", [128, C], BF16, 98304 + i * 2048) for i in range(8)]
    wgt = [sb(f"wgt{i}", [128, C], BF16, 114688 + i * 2048) for i in range(2)]
    wut = [sb(f"wut{i}", [128, C], BF16, 118784 + i * 2048) for i in range(2)]
    Pp = [sb("pph0", [128, 1024], BF16, 2048),
          sb("pph1", [128, 1024], BF16, 131072),
          sb("pph2", [128, 1024], BF16, 190528),
          sb("pph3", [128, 1024], BF16, 192576),
          sb("ppr0", [128, 1024], BF16, 122880),
          sb("ppr1", [128, 1024], BF16, 124928)]
    rrow = sb("rrow", [128, 512], F32, 126976)
    rrow2 = sb("rrow2", [128, 512], F32, 129024)
    Vhold = sb("vhold", [128, 520], BF16, 204928)
    rden = sb("rden", [128, 1024], F32, 194624)
    rtt = sb("rtt", [128, 512], F32, 206976)
    wd_sb = [sb(f"wd{i}", [128, C], BF16, FB + i * 2048) for i in range(24)]
    Ksb2 = [sb(f"ksb2{i}", [128, 1024], BF16, 182272 + i * 2048) for i in range(2)]
    Vsb2 = [sb(f"vsb2{i}", [128, 8 * 130], BF16, 186368 + i * 2080) for i in range(2)]
    gs = sb("gs", [128, 512], BF16, 198784)
    us = sb("us", [128, 512], BF16, 199808)
    tgs = sb("tgs", [128, 512], F32, 200832)
    trc = sb("trc", [128, 512], F32, 202880)

    ps_a = nc.alloc_psum_tensor("ps_a", [128, 512], F32)
    ps_b = nc.alloc_psum_tensor("ps_b", [128, 512], F32)
    ps_c = nc.alloc_psum_tensor("ps_c", [128, 512], F32)
    ps_d = nc.alloc_psum_tensor("ps_d", [128, 512], F32)
    ps_big = nc.alloc_psum_tensor("ps_big", [128, 2048], F32)

    S = Sched(nc)

    def lp(f):
        def g(e):
            with nc.allow_low_precision(reason="bf16 stream operand"):
                return f(e)
        return g

    def ckpt(name):
        if stop_after == name:
            raise _Trunc()

    try:
        S.op("vector", lambda e: e.memset(ones_bf[:, :], 1.0))
        S.op("vector", lambda e: e.memset(bones_bf[:, :], 0.0))
        S.op("vector", lambda e: e.memset(bones_bf[0:64, 0:64], 1.0))
        S.op("vector", lambda e: e.memset(bones_bf[64:128, 64:128], 1.0))
        S.op("vector", lambda e: e.memset(onesf32[:, :], 1.0))
        S.op("vector", lambda e: e.memset(
            Vstg[:].rearrange("p (n c) -> p n c", c=65)[:, :, 64:65], 1.0))
        S.op("vector", lambda e: e.memset(epsln_col[:, 0:1], LN_EPS))
        S.op("vector", lambda e: e.memset(negone_col[:, 0:1], -1.0))
        m_const = S.mark("vector")

        S.op("sync", lambda e: e.dma_start(
            out=qs8col[0:64, 0:1], in_=qscale[:].rearrange("(p o) -> p o", o=1)), dma=True)
        S.op("sync", lambda e: e.dma_start(
            out=qs8col[64:128, 0:1], in_=qscale[:].rearrange("(p o) -> p o", o=1)), dma=True)
        S.op("sync", lambda e: e.dma_start(
            out=kscol[0:64, 0:1], in_=kscale[:].rearrange("(p o) -> p o", o=1)), dma=True)
        S.op("sync", lambda e: e.dma_start(
            out=kscol[64:128, 0:1], in_=kscale[:].rearrange("(p o) -> p o", o=1)), dma=True)
        S.op("sync", lambda e: e.dma_start(
            out=bpcol[:, 0:8], in_=bproj[:].rearrange("(f p) -> p f", p=128),
            allow_slow_non_contiguous=True), dma=True)
        m_cols = S.mark("sync")
        S.op("vector", lambda e: e.tensor_scalar_mul(qs8col[:, 0:1], qs8col[:, 0:1], 0.125),
             deps=[m_cols])
        m_cols2 = S.mark("vector")

        S.op("vector", lambda e: e.scalar_tensor_tensor(
            bcol[:, 0:1], qs8col[:, 0:1], -1.0, qs8col[:, 0:1],
            op0=ALU.mult, op1=ALU.max), deps=[m_cols2])
        S.op("vector", lambda e: e.scalar_tensor_tensor(
            bcol2[:, 0:1], kscol[:, 0:1], -1.0, kscol[:, 0:1],
            op0=ALU.mult, op1=ALU.max), deps=[m_cols])
        m_abs = S.mark("vector")
        S.op("sync", lambda e: e.dma_start(out=brow[0:1, 0:128], in_=bcol[:, 0:1],
                                           allow_slow_non_contiguous=True),
             deps=[m_abs], dma=True)
        S.op("sync", lambda e: e.dma_start(out=brow[1:2, 0:128], in_=bcol2[:, 0:1],
                                           allow_slow_non_contiguous=True), dma=True)
        m_brow = S.mark("sync")
        S.op("vector", lambda e: e.reduce_max(tmpcol[0:2, 0:1], brow[0:2, 0:128],
                                              axis=AX.X), deps=[m_brow])
        m_red = S.mark("vector")
        S.op("sync", lambda e: e.dma_start(out=brow[0:1, 1:2], in_=tmpcol[1:2, 0:1]),
             deps=[m_red], dma=True)
        m_mv = S.mark("sync")
        S.op("vector", lambda e: e.tensor_mul(bcol2[0:1, 0:1], tmpcol[0:1, 0:1],
                                              brow[0:1, 1:2]), deps=[m_mv])
        S.op("vector", lambda e: e.tensor_scalar(
            bcol2[0:1, 0:1], bcol2[0:1, 0:1], -float(HD), -1.0,
            op0=ALU.mult, op1=ALU.add))
        m_b4 = S.mark("vector")
        S.op("tensor", lambda e: e.matmul(ps_a[:, 0:1], onesf32[0:1, 0:128],
                                          bcol2[0:1, 0:1], start=True, stop=True),
             deps=[m_b4, m_const])
        m_b5 = S.mark("tensor")
        S.op("vector", lambda e: e.tensor_copy(negB[:, 0:1], ps_a[:, 0:1]), deps=[m_b5])
        m_negB = S.mark("vector")
        stats_free = [S.mark("vector")]

        m_wq_cc = []
        for cc in range(CC):
            S.op("gpsimd", lambda e, cc=cc: e.dma_start(
                out=wqkv_sb[cc][:, :], in_=wqkv[cc * 128:(cc + 1) * 128, :]), dma=True)
            m_wq_cc.append(S.mark("gpsimd"))
        m_wqkv = m_wq_cc[-1]
        ckpt("P0")

        proj_banks = [ps_c[:, :], ps_big[:, 0:512], ps_big[:, 512:1024],
                      ps_big[:, 1024:1536]]
        proj_free = [[NONE_MARK] for _ in range(4)]
        bones_ms = ps_d[:, :]
        bones_bd = ps_big[:, 1536:2048]
        bones_ms_free = [NONE_MARK]
        bones_bd_free = [NONE_MARK]
        stats_free = [m_negB]
        rr = [0]
        ss = [0]
        slot_free = [{}, {}]

        def emit_ln(load_mark, xp):
            xb, xtmp = xb_pp[xp], xtmp_pp[xp]
            mrow, lnm2, lnvar = mrow_pp[xp], lnm2_pp[xp], lnvar_pp[xp]
            lnsd, rstd_b = lnsd_pp[xp], rstdb_pp[xp]
            S.op("scalar", lambda e: e.activation(xtmp[:, :], xb[:, :], AF.Square),
                 deps=[load_mark, m_const])
            m_sq = S.mark("scalar")
            for cc in range(CC):
                S.op("tensor", lambda e, cc=cc: e.matmul(
                    ps_a[:, :], ones_bf[:, :], xb[:, cc * 512:(cc + 1) * 512],
                    start=(cc == 0), stop=(cc == CC - 1)),
                    deps=[load_mark, m_const] + stats_free)
            for cc in range(CC):
                S.op("tensor", lambda e, cc=cc: e.matmul(
                    ps_b[:, :], ones_bf[:, :], xtmp[:, cc * 512:(cc + 1) * 512],
                    start=(cc == 0), stop=(cc == CC - 1)), deps=[m_sq])
            m_sums = S.mark("tensor")
            S.op("vector", lambda e: e.tensor_scalar_mul(mrow[:, :], ps_a[:, :], 1.0 / C),
                 deps=[m_sums])
            m_mrow = S.mark("vector")
            S.op("vector", lambda e: e.tensor_mul(lnm2[:, :], mrow[:, :], mrow[:, :]))
            S.op("vector", lambda e: e.scalar_tensor_tensor(
                lnvar[:, :], ps_b[:, :], 1.0 / C, lnm2[:, :],
                op0=ALU.mult, op1=ALU.subtract), deps=[m_sums])
            m_var = S.mark("vector")
            stats_free[:] = [m_var]
            S.op("scalar", lambda e: e.activation(
                lnsd[:, :], lnvar[:, :], AF.Sqrt, bias=epsln_col[:, 0:1]),
                deps=[m_var, m_const])
            m_sd = S.mark("scalar")
            S.op("vector", lp(lambda e: e.reciprocal(rstd_b[:, :], lnsd[:, :])),
                 deps=[m_sd])
            m_rstd = S.mark("vector")
            for cc in range(CC):
                sl = slice(cc * 512, (cc + 1) * 512)
                S.op("gpsimd", lambda e, sl=sl: e.tensor_sub(
                    xtmp[:, sl], xb[:, sl], mrow[:, :]), deps=[m_sums, m_mrow])
            m_cent = S.mark("gpsimd")
            for cc in range(CC):
                sl = slice(cc * 512, (cc + 1) * 512)
                S.op("vector", lambda e, sl=sl: e.tensor_mul(
                    xb[:, sl], xtmp[:, sl], rstd_b[:, :]),
                    deps=[m_cent, m_rstd])
            return S.mark("vector")

        def emit_rms(dst, scol, m_proj, pb, extra_out_deps=()):
            s = ss[0] % 2
            ss[0] += 1
            pp = proj_banks[pb]
            qsq, q2, sd, rc = qsq_pp[s], q2_pp[s], sd_pp[s], rc_pp[s]
            sf = slot_free[s]
            S.op("scalar", lambda e: e.activation(qsq[:, :], pp, AF.Square),
                 deps=[m_proj] + sf.get("qsq", []))
            m_cp = S.mark("scalar")
            S.op("tensor", lambda e: e.matmul(bones_ms, bones_bf[:, :], qsq[:, :],
                                              start=True, stop=True),
                 deps=[m_cp] + bones_ms_free)
            m_bo = S.mark("tensor")
            sf["qsq"] = [m_bo]
            S.op("scalar", lambda e: e.activation(
                sd[:, :], bones_ms, AF.Sqrt, scale=1.0 / HD),
                deps=[m_bo] + sf.get("sd", []))
            m_r = S.mark("scalar")
            bones_ms_free[:] = [m_r]
            S.op("vector", lambda e: e.tensor_scalar_add(rc[:, :], sd[:, :], RMS_EPS),
                 deps=[m_r])
            S.op("vector", lambda e: e.tensor_scalar_mul(sd[:, :], pp, scol[:, 0:1]),
                 deps=[m_proj])
            m_tmp = S.mark("vector")
            S.op("vector", lambda e: e.reciprocal(rscr[:, :], rc[:, :]))
            S.op("vector", lambda e, dst=dst: e.tensor_mul(dst, sd[:, :], rscr[:, :]),
                 deps=list(extra_out_deps))
            m_dst = S.mark("vector")
            sf["sd"] = [m_dst]
            proj_free[pb][:] = [m_cp, m_tmp]
            return m_dst

        xb_consumed = [[], []]
        pendq = None
        def emit_load(t0, src, xp, extra=()):
            S.op("gpsimd", lambda e, t0=t0, src=src, xp=xp: e.dma_start(
                out=xb_pp[xp][:].rearrange("p (cc t) -> p cc t", cc=CC),
                in_=src[:, t0:t0 + 512].rearrange("(cc p) t -> p cc t", p=128)),
                deps=xb_consumed[xp] + list(extra), dma=True)
            return S.mark("gpsimd")

        qstg_last = {}

        def flush_q(pq):
            pfc, pt0, ppar, pm_proj, ppb = pq
            extra = [qstg_last[ppar]] if ppar in qstg_last else []
            m_dst = emit_rms(Qstg[ppar][:, :], qs8col, pm_proj, ppb,
                             extra_out_deps=extra)
            S.op("sync", lambda e, pfc=pfc, pt0=pt0, ppar=ppar: e.dma_start(
                out=qsp[pfc, :, pt0:pt0 + 512], in_=Qstg[ppar][:, :]),
                deps=[m_dst], dma=True)
            qstg_last[ppar] = S.mark("sync")

        load_marks = {}
        load_marks[0] = emit_load(0, xq, 0)
        for tc in range(TQ // 512):
            t0 = tc * 512
            xp = tc % 2
            if tc + 1 < TQ // 512:
                load_marks[tc + 1] = emit_load(t0 + 512, xq, (tc + 1) % 2)
            m_xn = emit_ln(load_marks[tc], xp)
            xb = xb_pp[xp]
            for fc in range(CC):
                pb = rr[0] % 4
                rr[0] += 1
                for cc in range(CC):
                    S.op("tensor", lambda e, cc=cc, fc=fc, pb=pb, xb=xb: e.matmul(
                        proj_banks[pb], wqkv_sb[cc][:, fc * 128:(fc + 1) * 128],
                        xb[:, cc * 512:(cc + 1) * 512],
                        start=(cc == 0), stop=(cc == CC - 1)),
                        deps=[m_xn, m_wq_cc[cc]] + proj_free[pb])
                m_proj = S.mark("tensor")
                if pendq is not None:
                    flush_q(pendq)
                pendq = (fc, t0, fc % 2, m_proj, pb)
            xb_consumed[xp] = [S.mark("tensor"), S.mark("vector"), S.mark("gpsimd"),
                               S.mark("scalar")]
        flush_q(pendq)
        pendq = None

        ckpt("P1")

        kstg_last = {}
        vstg_last = {}
        load_marks = {}
        load_marks[0] = emit_load(0, xT, 0, extra=xb_consumed[0])
        for tc in range(N // 512):
            t0 = tc * 512
            xp = tc % 2
            if tc + 1 < N // 512:
                ex = xb_consumed[(tc + 1) % 2] if tc == 0 else ()
                load_marks[tc + 1] = emit_load(t0 + 512, xT, (tc + 1) % 2, extra=ex)
            m_xn = emit_ln(load_marks[tc], xp)
            xb = xb_pp[xp]
            pendk = None

            def flush_k(pk):
                pfc, pt0, ppar, pm_proj, ppb = pk
                extra = [kstg_last[ppar]] if ppar in kstg_last else []
                m_dst = emit_rms(Kstg[ppar][:, :], kscol, pm_proj, ppb,
                                 extra_out_deps=extra)
                S.op("sync", lambda e, pfc=pfc, pt0=pt0, ppar=ppar: e.dma_start(
                    out=ksp[pfc, :, pt0:pt0 + 512], in_=Kstg[ppar][:, :]),
                    deps=[m_dst], dma=True)
                kstg_last[ppar] = S.mark("sync")

            for fc in range(CC):
                pb = rr[0] % 4
                rr[0] += 1
                for cc in range(CC):
                    S.op("tensor", lambda e, cc=cc, fc=fc, pb=pb, xb=xb: e.matmul(
                        proj_banks[pb], wqkv_sb[cc][:, C + fc * 128: C + (fc + 1) * 128],
                        xb[:, cc * 512:(cc + 1) * 512],
                        start=(cc == 0), stop=(cc == CC - 1)),
                        deps=[m_xn] + proj_free[pb])
                m_proj = S.mark("tensor")
                if pendk is not None:
                    flush_k(pendk)
                pendk = (fc, t0, fc % 2, m_proj, pb)
            flush_k(pendk)
            for vg in range(2):
                for tt in range(4):
                    kt = tc * 4 + tt
                    c0 = 2 * C + vg * 512
                    pb = rr[0] % 4
                    rr[0] += 1
                    for cc in range(CC):
                        S.op("tensor", lambda e, cc=cc, c0=c0, tt=tt, pb=pb, xb=xb: e.matmul(
                            proj_banks[pb],
                            xb[:, cc * 512 + tt * 128: cc * 512 + tt * 128 + 128],
                            wqkv_sb[cc][:, c0:c0 + 512],
                            start=(cc == 0), stop=(cc == CC - 1)),
                            deps=[m_xn] + proj_free[pb])
                    m_v = S.mark("tensor")
                    parity = (tt // 2) % 2
                    half = tt % 2
                    dep_sp = [m for r in range(4)
                              for m in ([vstg_last[(r, parity)]]
                                        if (r, parity) in vstg_last else [])]
                    out5 = Vstg[:, parity * 1040:(parity + 1) * 1040].rearrange(
                        "p (r hf h c) -> p r hf h c", r=4, hf=2, h=2)[
                        :, :, half:half + 1, :, 0:64]
                    S.op("scalar", lambda e, out5=out5, pb=pb: e.copy(
                        out=out5,
                        in_=proj_banks[pb].rearrange(
                            "p (r o h c) -> p r o h c", r=4, o=1, h=2)),
                        deps=[m_v, m_const] + dep_sp)
                    m_cp = S.mark("scalar")
                    proj_free[pb][:] = [m_cp]
                    if half == 1:
                        for r in range(4):
                            stg0 = parity * 4 + r
                            reg = 4 * vg + r
                            S.op("sync", lambda e, reg=reg, kt=kt, stg0=stg0: e.dma_start(
                                out=vsp[reg, :, (kt - 1) * 130:(kt + 1) * 130],
                                in_=Vstg[:, stg0 * 260: stg0 * 260 + 260]),
                                deps=[m_cp], dma=True)
                            vstg_last[(r, parity)] = S.mark("sync")
            xb_consumed[xp] = [S.mark("tensor"), S.mark("vector"), S.mark("gpsimd"),
                               S.mark("scalar")]

        ckpt("P2")
        m_kspill = S.mark("sync")
        m_vspill = S.mark("sync")

        ckpt("B")
        S.barrier()

        for cc in range(CC):
            S.op("gpsimd", lambda e, cc=cc: e.dma_start(
                out=wproj_sb[cc][:, :], in_=wproj[cc * 128:(cc + 1) * 128, :]), dma=True)
        m_wproj = S.mark("gpsimd")
        for i in range(24):
            S.op("gpsimd", lambda e, i=i: e.dma_start(
                out=wd_sb[i][:, :], in_=wdT[i * 128:(i + 1) * 128, :]), dma=True)
        m_wd = S.mark("gpsimd")

        tbank = [ps_c, ps_d]
        tb_free = [[NONE_MARK], [NONE_MARK]]
        ao_done = {}
        ao_free = {0: [NONE_MARK], 1: [NONE_MARK]}
        d1_free = [NONE_MARK]
        xr_free = [NONE_MARK]
        xn_free = [NONE_MARK]
        hact_free = [NONE_MARK]
        r1b_free = [NONE_MARK]
        m_tgs_rd = [NONE_MARK]
        m_us_rd = [NONE_MARK]
        wg_last = {}

        def tail_gen(qcp, last=False):
            slot = qcp % 2
            t0 = qcp * 512
            S.op("sync", lambda e, t0=t0: e.dma_start(
                out=xr[:].rearrange("p (cc t) -> p cc t", cc=CC),
                in_=xq[:, t0:t0 + 512].rearrange("(cc p) t -> p cc t", p=128)),
                deps=[xr_free[0]], dma=True)
            m_xr = S.mark("sync")
            yield
            d1w = d1_free[0]
            for fo in range(CC):
                bk = fo % 2
                for cc in range(CC):
                    S.op("tensor", lambda e, cc=cc, fo=fo, bk=bk, slot=slot: e.matmul(
                        tbank[bk][:, :], wproj_sb[cc][:, fo * 128:(fo + 1) * 128],
                        attnout2[slot][cc][:, :],
                        start=(cc == 0), stop=(cc == CC - 1)),
                        deps=[m_wproj, ao_done[qcp]] + tb_free[bk])
                    if cc % 2 == 1:
                        yield
                m_p = S.mark("tensor")
                S.op("vector", lambda e, fo=fo, bk=bk: e.tensor_scalar_add(
                    delta1v[fo][:, :], tbank[bk][:, :], bpcol[:, fo:fo + 1]),
                    deps=[m_p, m_cols, d1w])
                tb_free[bk][:] = [S.mark("vector")]
            ao_free[slot][:] = [S.mark("tensor")]
            m_d1 = S.mark("vector")
            for cc in range(CC):
                sl = slice(cc * 512, (cc + 1) * 512)
                S.op("vector", lambda e, cc=cc, sl=sl: e.tensor_add(
                    r1b[:, sl], xr[:, sl], delta1v[cc][:, :]),
                    deps=[m_xr, m_d1, r1b_free[0]])
            m_r1 = S.mark("vector")
            for cc in range(CC):
                S.op("tensor", lambda e, cc=cc: e.matmul(
                    ps_c[:, :], ones_bf[:, :], r1b[:, cc * 512:(cc + 1) * 512],
                    start=(cc == 0), stop=(cc == CC - 1)),
                    deps=[m_r1] + tb_free[0])
                if cc % 2 == 1:
                    yield
            m_sum1 = S.mark("tensor")
            S.op("vector", lambda e: e.tensor_mul(sq2[:, :], r1b[:, :], r1b[:, :]),
                 deps=[m_r1, hact_free[0]])
            m_sq = S.mark("vector")
            yield
            for cc in range(CC):
                S.op("tensor", lambda e, cc=cc: e.matmul(
                    ps_d[:, :], ones_bf[:, :], sq2[:, cc * 512:(cc + 1) * 512],
                    start=(cc == 0), stop=(cc == CC - 1)),
                    deps=[m_sq] + tb_free[1])
                if cc % 2 == 1:
                    yield
            m_sums = S.mark("tensor")
            S.op("vector", lambda e: e.tensor_scalar_mul(m2[:, :], ps_c[:, :], 1.0 / C),
                 deps=[m_sums, m_sum1])
            S.op("vector", lambda e: e.tensor_mul(mm2[:, :], m2[:, :], m2[:, :]))
            S.op("vector", lambda e: e.scalar_tensor_tensor(
                var2[:, :], ps_d[:, :], 1.0 / C, mm2[:, :],
                op0=ALU.mult, op1=ALU.subtract), deps=[m_sums])
            m_var = S.mark("vector")
            tb_free[0][:] = [m_var]
            tb_free[1][:] = [m_var]
            S.op("scalar", lambda e: e.activation(sd2[:, :], var2[:, :], AF.Sqrt,
                                                  bias=epsln_col[:, 0:1]), deps=[m_var])
            m_sd = S.mark("scalar")
            S.op("vector", lambda e: e.reciprocal(rstd2[:, :], sd2[:, :]), deps=[m_sd])
            S.op("vector", lambda e: e.tensor_scalar_mul(rstd2[:, :], rstd2[:, :], DQ))
            m_rstd = S.mark("vector")
            xnw = xn_free[0]
            for cc in range(CC):
                sl = slice(cc * 512, (cc + 1) * 512)
                S.op("vector", lambda e, sl=sl: e.tensor_sub(
                    sq2[:, sl], r1b[:, sl], m2[:, :]), deps=[m_rstd])
                S.op("vector", lambda e, cc=cc, sl=sl: e.tensor_mul(
                    xn2v[cc][:, :], sq2[:, sl], rstd2[:, :]), deps=[xnw])
            m_xn2 = S.mark("vector")
            r1b_free[0] = m_xn2
            yield
            hact_w = [NONE_MARK] * HC
            for hc in range(HC):
                buf = hc % 2
                ld = [wg_last[buf]] if buf in wg_last else []
                S.op("gpsimd", lambda e, hc=hc, buf=buf: e.dma_start(
                    out=wgt[buf][:].rearrange("p (cc h) -> p cc h", cc=CC),
                    in_=wgT[:, hc * 128:(hc + 1) * 128].rearrange(
                        "(cc p) h -> p cc h", p=128)), deps=ld, dma=True)
                S.op("gpsimd", lambda e, hc=hc, buf=buf: e.dma_start(
                    out=wut[buf][:].rearrange("p (cc h) -> p cc h", cc=CC),
                    in_=wuT[:, hc * 128:(hc + 1) * 128].rearrange(
                        "(cc p) h -> p cc h", p=128)), dma=True)
                m_w = S.mark("gpsimd")
                for cc in range(CC):
                    S.op("tensor", lambda e, cc=cc, buf=buf: e.matmul(
                        ps_c[:, :], wgt[buf][:, cc * 128:(cc + 1) * 128],
                        xn2v[cc][:, :], start=(cc == 0), stop=(cc == CC - 1)),
                        deps=[m_w, m_xn2] + tb_free[0])
                    if cc % 2 == 1:
                        yield
                m_g = S.mark("tensor")
                for cc in range(CC):
                    S.op("tensor", lambda e, cc=cc, buf=buf: e.matmul(
                        ps_d[:, :], wut[buf][:, cc * 128:(cc + 1) * 128],
                        xn2v[cc][:, :], start=(cc == 0), stop=(cc == CC - 1)),
                        deps=[m_w] + tb_free[1])
                    if cc % 2 == 1:
                        yield
                m_u = S.mark("tensor")
                wg_last[buf] = m_u
                if last:
                    S.op("scalar", lambda e: e.activation(
                        gs[:, :], ps_c[:, :], AF.Silu),
                        deps=[m_g, m_tgs_rd[0]])
                    m_sl = S.mark("scalar")
                    tb_free[0][:] = [m_sl]
                    S.op("scalar", lambda e: e.activation(
                        us[:, :], ps_d[:, :], AF.Copy, scale=DQ),
                        deps=[m_u, m_tgs_rd[0]])
                    m_us2 = S.mark("scalar")
                    tb_free[1][:] = [m_us2]
                    S.op("vector", lambda e, hc=hc: e.tensor_mul(
                        hact[hc][:, :], gs[:, :], us[:, :]),
                        deps=[m_us2, hact_free[0]])
                    hact_w[hc] = S.mark("vector")
                    m_tgs_rd[0] = hact_w[hc]
                    m_us_rd[0] = hact_w[hc]
                else:
                    S.op("vector", lp(lambda e: e.tensor_copy(gs[:, :], ps_c[:, :])),
                         deps=[m_g, m_us_rd[0]])
                    m_gc = S.mark("vector")
                    tb_free[0][:] = [m_gc]
                    S.op("vector", lp(lambda e: e.tensor_scalar_mul(
                        us[:, :], ps_d[:, :], DQ)), deps=[m_u])
                    tb_free[1][:] = [S.mark("vector")]
                    S.op("scalar", lambda e: e.activation(tgs[:, :], gs[:, :], AF.Exp,
                                                          scale=-1.0),
                         deps=[m_gc, m_tgs_rd[0]])
                    m_e = S.mark("scalar")
                    S.op("vector", lambda e: e.tensor_scalar_add(
                        tgs[:, :], tgs[:, :], 1.0), deps=[m_e])
                    S.op("vector", lambda e: e.reciprocal(trc[:, :], tgs[:, :]))
                    S.op("vector", lambda e: e.tensor_mul(tgs[:, :], gs[:, :],
                                                          trc[:, :]))
                    S.op("vector", lambda e, hc=hc: e.tensor_mul(
                        hact[hc][:, :], tgs[:, :], us[:, :]),
                        deps=[hact_free[0]])
                    hact_w[hc] = S.mark("vector")
                    m_tgs_rd[0] = hact_w[hc]
                    m_us_rd[0] = m_e
            for dc in range(CC):
                bk = dc % 2
                for hc in range(HC):
                    S.op("tensor", lambda e, hc=hc, dc=dc, bk=bk: e.matmul(
                        tbank[bk][:, :], wd_sb[hc][:, dc * 128:(dc + 1) * 128],
                        hact[hc][:, :], start=(hc == 0), stop=(hc == HC - 1)),
                        deps=[m_wd, hact_w[hc]] + tb_free[bk])
                    if hc % 2 == 1:
                        yield
                m_dn = S.mark("tensor")
                S.op("vector", lambda e, dc=dc: e.tensor_add(
                    m2[:, :], xr[:, dc * 512:(dc + 1) * 512], delta1v[dc][:, :]),
                    deps=[m_dn])
                S.op("vector", lambda e, bk=bk: e.tensor_add(
                    mm2[:, :], tbank[bk][:, :], m2[:, :]))
                m_o = S.mark("vector")
                tb_free[bk][:] = [m_o]
                S.op("sync", lambda e, dc=dc, t0=t0: e.dma_start(
                    out=out[dc * 128:(dc + 1) * 128, t0:t0 + 512], in_=mm2[:, :]),
                    deps=[m_o], out=True)
                S.wait_only("vector", [S.out_mark()])
            m_te_v = S.mark("vector")
            m_te_t = S.mark("tensor")
            d1_free[0] = m_te_v
            xr_free[0] = m_te_v
            xn_free[0] = m_te_t
            hact_free[0] = m_te_t

        def drain(g, n):
            if g is None:
                return None
            for _ in range(n):
                try:
                    next(g)
                except StopIteration:
                    return None
            return g

        NCK = 8
        CHUNKS = [(qcx, fcx, ckx) for qcx in range(4) for fcx in range(8)
                  for ckx in range(NCK)]
        kfree = [[NONE_MARK], [NONE_MARK]]
        vfree = [[NONE_MARK], [NONE_MARK]]
        qsb_free = [[NONE_MARK], [NONE_MARK]]
        loadmark = {}

        def issue_load(g):
            if g >= len(CHUNKS) or g in loadmark:
                return
            _, fcx, ckx = CHUNKS[g]
            b = g % 2
            S.op("sync", lambda e, fcx=fcx, ckx=ckx, b=b: e.dma_start(
                out=Ksb2[b][:, :], in_=ksp[fcx, :, ckx * 1024:(ckx + 1) * 1024]),
                deps=[m_kspill] + kfree[b], dma=True)
            S.op("sync", lambda e, fcx=fcx, ckx=ckx, b=b: e.dma_start(
                out=Vsb2[b][:, :], in_=vsp[fcx, :, ckx * 1040:(ckx + 1) * 1040]),
                deps=[m_vspill] + vfree[b], dma=True)
            loadmark[g] = S.mark("sync")

        exp_mark = {0: NONE_MARK, 1: NONE_MARK}
        pp_free = {i: NONE_MARK for i in range(6)}
        evac_ref = [NONE_MARK]
        stash = []
        vhold_free = [NONE_MARK]
        issue_load(0)
        issue_load(1)

        def pslot(kt):
            return kt if kt < 4 else 4 + (kt % 2)

        gen = None
        for qc in range(4):
            slot = qc % 2
            if qc > 0:
                gen = tail_gen(qc - 1)
            for fc in range(8):
                g0 = (qc * 8 + fc) * NCK
                qb = fc % 2
                S.op("sync", lambda e, fc=fc, qc=qc, qb=qb: e.dma_start(
                    out=Qsb[qb][:, :], in_=qsp[fc, :, qc * 512:(qc + 1) * 512]),
                    deps=[m_kspill] + qsb_free[qb], dma=True)
                S.op("sync", lambda e, fc=fc: e.dma_start(
                    out=Vhold[:, :], in_=vsp[fc, :, 0:520]),
                    deps=[m_vspill, vhold_free[0]], dma=True)
                m_q = S.mark("sync")
                exps = [NONE_MARK] * NKT

                def emit_av(ktav, deps_extra=()):
                    ps = pslot(ktav)
                    bv = (g0 + ktav // NCK) % 2
                    vo = (ktav % NCK) * 130
                    vsrc = Vhold if ktav < 4 else Vsb2[bv]
                    if ktav < 4:
                        vo = ktav * 130
                    S.op("tensor", lambda e, vsrc=vsrc, vo=vo, ktav=ktav, ps=ps: e.matmul(
                        ps_a[0:65, :], vsrc[:, vo:vo + 65],
                        Pp[ps][:, 0:512], start=(ktav == 4), stop=(ktav == 3)),
                        deps=[exps[ktav], loadmark[g0 + ktav // NCK]]
                        + list(deps_extra))
                    S.op("tensor", lambda e, vsrc=vsrc, vo=vo, ktav=ktav, ps=ps: e.matmul(
                        ps_b[0:65, :], vsrc[:, vo + 65:vo + 130],
                        Pp[ps][:, 512:1024], start=(ktav == 4), stop=(ktav == 3)))
                    m = S.mark("tensor")
                    pp_free[ps] = m
                    if ktav % NCK == NCK - 1:
                        vfree[bv][:] = [m]
                    return m

                for kt in range(NKT):
                    ck = kt // NCK
                    b = (g0 + ck) % 2
                    half = kt % 2
                    hb = half * 1024
                    ko = (kt % NCK) * 128
                    S.op("tensor", lambda e, b=b, ko=ko, hb=hb, qb=qb: e.matmul(
                        ps_big[:, hb:hb + 512],
                        Ksb2[b][0:64, ko:ko + 128],
                        Qsb[qb][0:64, :], start=True, stop=True),
                        deps=[m_q, loadmark[g0 + ck], exp_mark[half]])
                    S.op("tensor", lambda e, b=b, ko=ko, hb=hb, qb=qb: e.matmul(
                        ps_big[:, hb + 512:hb + 1024],
                        Ksb2[b][64:128, ko:ko + 128],
                        Qsb[qb][64:128, :], start=True, stop=True))
                    m_st = S.mark("tensor")
                    if kt % NCK == NCK - 1:
                        kfree[b][:] = [m_st]
                    if kt in (4, 5) and stash:
                        stash.pop(0)()
                    if kt >= 5:
                        emit_av(kt - 1,
                                deps_extra=([evac_ref[0]] if kt - 1 == 4 else ()))
                    if kt % NCK == 0:
                        issue_load(g0 + ck + 1)
                    gen = drain(gen, 1)
                    S.op("scalar", lambda e, ps=pslot(kt), hb=hb: e.activation(
                        Pp[ps][:, :], ps_big[:, hb:hb + 1024], AF.Exp,
                        bias=negB[:, 0:1]), deps=[m_st, pp_free[pslot(kt)]])
                    exp_mark[half] = S.mark("scalar")
                    exps[kt] = exp_mark[half]
                emit_av(NKT - 1)
                emit_av(0)
                emit_av(1)
                emit_av(2)
                m_av = emit_av(3)
                m_vh = S.mark("tensor")
                vhold_free[0] = m_vh
                qsb_free[qb][:] = [m_vh]
                S.op("vector", lambda e: e.tensor_copy(
                    rrow[64:65, :], ps_a[64:65, :]), deps=[m_av])
                S.op("vector", lambda e: e.tensor_copy(
                    rrow2[64:65, :], ps_b[64:65, :]))
                for u in range(2):
                    bank = (ps_a, ps_b)[u]
                    rbuf = (rrow, rrow2)[u]
                    rs = slice(u * 512, u * 512 + 512)
                    S.op("vector", lambda e, rbuf=rbuf, rs=rs: e.reciprocal(
                        rden[64:65, rs], rbuf[64:65, :]))
                    S.op("vector", lambda e, bank=bank, rbuf=rbuf: e.tensor_copy(
                        rbuf[0:64, :], bank[0:64, :]))
                    m_rc = S.mark("vector")

                    def mk_bc(u=u, bank=bank, rbuf=rbuf, rs=rs, m_rc=m_rc, fc=fc,
                              slot=slot, aof=list(ao_free[slot])):
                        def emit():
                            S.op("tensor", lambda e, rs=rs: e.matmul(
                                bank[0:64, :], onesf32[64:65, 0:64],
                                rden[64:65, rs], start=True, stop=True),
                                deps=[m_rc])
                            m_bc = S.mark("tensor")
                            S.op("vector", lambda e: e.tensor_mul(
                                attnout2[slot][fc][u * 64:(u + 1) * 64, :],
                                rbuf[0:64, :], bank[0:64, :]),
                                deps=[m_bc] + aof)
                            evac_ref[0] = S.mark("vector")
                        return emit
                    stash.append(mk_bc())
            while stash:
                stash.pop(0)()
            ao_done[qc] = S.mark("vector")
            ckpt("P3q%d" % qc)
        while gen is not None:
            gen = drain(gen, 16)
        gen = tail_gen(3, last=True)
        while gen is not None:
            gen = drain(gen, 16)
        ckpt("P6")


    except _Trunc:
        pass
    S.final_out_wait("sync")
    S.build_block()
    return nc


_NC = None


def kernel(L0_attn_x, L0_attn_Wqkv, L0_attn_Wproj, L0_attn_bproj,
           L0_attn_q_scale, L0_attn_k_scale,
           L0_ffn_w_gate, L0_ffn_w_up, L0_ffn_w_down):
    global _NC
    x = np.asarray(L0_attn_x, dtype=np.float32)
    wqkv_h = np.ascontiguousarray(np.asarray(L0_attn_Wqkv, dtype=np.float32))
    wproj_h = np.ascontiguousarray(np.asarray(L0_attn_Wproj, dtype=np.float32))
    bproj_h = np.ascontiguousarray(np.asarray(L0_attn_bproj, dtype=np.float32))
    qs_h = np.ascontiguousarray(np.asarray(L0_attn_q_scale, dtype=np.float32))
    ks_h = np.ascontiguousarray(np.asarray(L0_attn_k_scale, dtype=np.float32))
    wgT_h = np.ascontiguousarray(np.asarray(L0_ffn_w_gate, dtype=np.int32).T)
    wuT_h = np.ascontiguousarray(np.asarray(L0_ffn_w_up, dtype=np.int32).T)
    wdT_h = np.ascontiguousarray(np.asarray(L0_ffn_w_down, dtype=np.int32).T)
    xT_h = np.ascontiguousarray(x.transpose(0, 2, 1))

    if _NC is None:
        _NC = build_program()

    in_maps = []
    for c in range(8):
        b, q = c // 4, c % 4
        in_maps.append({
            "xT": xT_h[b],
            "xq": np.ascontiguousarray(xT_h[b][:, q * TQ:(q + 1) * TQ]),
            "wqkv": wqkv_h, "wproj": wproj_h, "bproj": bproj_h,
            "qscale": qs_h, "kscale": ks_h,
            "wgT": wgT_h, "wuT": wuT_h, "wdT": wdT_h,
        })
    res = run_bass_kernel_spmd(_NC, in_maps, core_ids=list(range(8)))
    outT = np.empty((B, C, N), dtype=np.float32)
    for c in range(8):
        b, q = c // 4, c % 4
        outT[b][:, q * TQ:(q + 1) * TQ] = np.asarray(res.results[c]["out"])
    return np.ascontiguousarray(outT.transpose(0, 2, 1))
